# version 10 (final)
# baseline (speedup 1.0000x reference)
import os as _os
_os.environ.setdefault("GAUGE_NTFF_IGNORE_DMA", "1")
import numpy as np
import ml_dtypes
import concourse.bass as bass
import concourse.mybir as mybir
from concourse.bass_utils import run_bass_kernel_spmd

F32 = mybir.dt.float32
BF16 = mybir.dt.bfloat16
AF = mybir.ActivationFunctionType
ALU = mybir.AluOpType
AX = mybir.AxisListType.X

D = 1024
TC = 512
CR = 256
RNDC = 12582912.0
NEGBIG = -1e30
EPS5 = 1e-5

ENGS = ("sync", "scalar", "vector", "tensor", "gpsimd")
ALL_PHASES = ("s1", "s2", "s3", "s4", "s5", "s6", "s7", "s8")


def host_layout(inputs, lane):
    g = lambda k: np.ascontiguousarray(np.asarray(inputs[k]))
    d = {}
    d["xin"] = np.ascontiguousarray(np.asarray(inputs["L0_attn_a_x"])[lane])
    A = np.asarray(inputs["L0_attn_a_A"])
    at = np.empty((8, 128, 1024), np.float32)
    for k in range(8):
        at[k] = A[:, k * 128:(k + 1) * 128].T
    d["s1_at"] = np.ascontiguousarray(at)
    d["s1_winT"] = g("L0_attn_a_W_in").T.copy()
    d["s1_wgateT"] = g("L0_attn_a_W_gate").T.copy()
    d["s1_bT"] = g("L0_attn_a_B").T.copy()
    d["s1_woutT"] = g("L0_attn_a_W_out").T.copy()
    d["s1_bin"] = g("L0_attn_a_b_in").reshape(8, 128, 1).copy()
    d["s1_bgate"] = g("L0_attn_a_b_gate").reshape(8, 128, 1).copy()
    d["s1_bout"] = g("L0_attn_a_b_out").reshape(1, 1024).copy()
    d["f_w1T"] = g("L0_ffn_w1").T.copy()
    d["f_w2T"] = g("L0_ffn_w2").T.copy()
    d["f_b1"] = g("L0_ffn_b1").reshape(1, 4096).copy()
    d["f_b2"] = g("L0_ffn_b2").reshape(1, 1024).copy()
    d["f_ln1w"] = g("L0_ffn_ln1_w").reshape(1, 1024).copy()
    d["f_ln1b"] = g("L0_ffn_ln1_b").reshape(1, 1024).copy()
    d["f_ln2w"] = g("L0_ffn_ln2_w").reshape(1, 4096).copy()
    d["f_ln2b"] = g("L0_ffn_ln2_b").reshape(1, 4096).copy()
    qkv_w = g("L0_attn_b_qkv_w")
    gate_w = g("L0_attn_b_gate_w")
    d["la_wcat"] = np.concatenate([qkv_w, gate_w], axis=1).copy()
    qkv_b = g("L0_attn_b_qkv_b")
    d["la_qb"] = qkv_b[0:1024].reshape(8, 128, 1).copy()
    d["la_kb"] = qkv_b[1024:2048].reshape(8, 128, 1).copy()
    d["la_vb_row"] = qkv_b[2048:3072].reshape(1, 1024).copy()
    d["la_gb"] = g("L0_attn_b_gate_b").reshape(8, 128, 1).copy()
    d["la_lng"] = g("L0_attn_b_ln_g").reshape(1, 1024).copy()
    d["la_lnb"] = g("L0_attn_b_ln_b").reshape(1, 1024).copy()
    d["sw_winT"] = g("L1_attn_a_w_in").T.copy()
    d["sw_woutT"] = g("L1_attn_a_w_out").T.copy()
    d["q_rhs"] = g("L1_ffn_rhs").copy()
    d["q_rhsT"] = g("L1_ffn_rhs").T.copy()
    d["s7_qkv"] = g("L1_attn_b_qkv_w").copy()
    qb7 = g("L1_attn_b_qkv_b")
    d["s7_qb"] = qb7[0:1024].reshape(8, 128, 1).copy()
    d["s7_kb"] = qb7[1024:2048].reshape(8, 128, 1).copy()
    d["s7_vb_row"] = qb7[2048:3072].reshape(1, 1024).copy()
    d["s7_rq"] = g("L1_attn_b_rq_w").copy()
    d["s7_rqb"] = g("L1_attn_b_rq_b").reshape(2, 128, 1).copy()
    d["s7_rkv"] = g("L1_attn_b_rkv_w").copy()
    d["s7_rkvb"] = g("L1_attn_b_rkv_b").reshape(4, 128, 1).copy()
    d["s7_rout"] = g("L1_attn_b_rout_w").copy()
    d["s7_routb_row"] = g("L1_attn_b_rout_b").reshape(1, 1024).copy()
    d["s7_out"] = g("L1_attn_b_out_w").copy()
    d["s7_outb_row"] = g("L1_attn_b_out_b").reshape(1, 1024).copy()
    gl = g("L1_attn_b_gl_w").reshape(64)
    gr = g("L1_attn_b_gr_w").reshape(64)
    d["s7_glgr_row"] = np.concatenate([np.tile(gl, 16), np.tile(gr, 16)]).reshape(1, 2048).copy()
    return d


def host_consts(S):
    c = {}
    c["ones_row"] = np.ones((1, 128), np.float32)
    c["triu128"] = np.triu(np.ones((128, 128), np.float32))
    WIN = 512
    qi = np.arange(WIN)[:, None]
    kj = np.arange(2 * WIN)[None, :]
    delta = (qi - kj + WIN).astype(np.float32)
    valid = (delta >= 0) & (delta <= WIN)
    tab1 = np.where(valid, delta, NEGBIG).astype(np.float32)
    tab0 = np.where(valid & (kj >= WIN), delta, NEGBIG).astype(np.float32)
    c["sw_tab"] = np.ascontiguousarray(np.stack([tab0, tab1]))
    ql = np.arange(256)[:, None]
    idx = np.arange(384)[None, :]
    dist = ql - idx + 128
    allowed = (dist > 0) & (dist <= 128)
    t0_ = np.where(allowed & (idx >= 128), 0.0, NEGBIG).astype(np.float32)
    t1_ = np.where(allowed, 0.0, NEGBIG).astype(np.float32)
    c["loc_tab"] = np.ascontiguousarray(np.stack([t0_, t1_]))
    c["lat_iota"] = np.ascontiguousarray(
        np.broadcast_to(np.arange(64, dtype=np.float32)[None, :], (128, 64)).copy())
    return c


class B:

    def __init__(self):
        self.q = {e: [] for e in ENGS}
        self.cnt = {}
        self.nbar = 0

    def raw(self, eng, fn):
        self.q[eng].append(fn)

    def op(self, eng, fn, sem=None, n=1):
        if sem is None:
            self.q[eng].append(fn)
            return self.cnt
        self.cnt[sem.name] = self.cnt.get(sem.name, 0) + n
        def run(e, fn=fn, sem=sem, n=n):
            fn(e).then_inc(sem, n)
        self.q[eng].append(run)
        return self.cnt[sem.name]

    def dma(self, eng, sem, out, in_, **kw):
        return self.op(eng, lambda e: e.dma_start(out=out, in_=in_, **kw), sem, 16)

    def dma_tr(self, eng, sem, out, in_):
        return self.op(eng, lambda e: e.dma_start_transpose(out, in_), sem, 16)

    def wait(self, eng, sem, val=None):
        v = self.cnt.get(sem.name, 0) if val is None else val
        self.q[eng].append(lambda e, sem=sem, v=v: e.wait_ge(sem, v))
        return v

    def barrier(self, PH, engs=ENGS):
        self.nbar += 1
        tgt = self.nbar * len(engs)
        for eng in engs:
            self.q[eng].append(lambda e: e.drain())
            self.q[eng].append(lambda e, PH=PH: e.sem_inc(PH, 1))
        for eng in engs:
            self.q[eng].append(lambda e, PH=PH, tgt=tgt: e.wait_ge(PH, tgt))
        self.cnt[PH.name] = tgt

    def replay(self, block):
        for eng in ENGS:
            getattr(block, eng)(lambda e, eng=eng: [f(e) for f in self.q[eng]])


def build_nc(S=8192, phases=ALL_PHASES):
    NT = S // TC
    nc = bass.Bass("TRN2", target_bir_lowering=False)

    def I(name, shape, dtype=F32):
        return nc.dram_tensor(name, list(shape), dtype, kind="ExternalInput").ap()

    xin = I("xin", [S, D])
    s1_at = I("s1_at", [8, 128, 1024])
    s1_winT = I("s1_winT", [D, D])
    s1_wgateT = I("s1_wgateT", [D, D])
    s1_bT = I("s1_bT", [D, D])
    s1_woutT = I("s1_woutT", [D, D])
    s1_bin = I("s1_bin", [8, 128, 1])
    s1_bgate = I("s1_bgate", [8, 128, 1])
    s1_bout = I("s1_bout", [1, D])
    ones_row = I("ones_row", [1, 128])
    f_w1T = I("f_w1T", [1024, 4096])
    f_w2T = I("f_w2T", [4096, 1024])
    f_b1 = I("f_b1", [1, 4096])
    f_b2 = I("f_b2", [1, 1024])
    f_ln1w = I("f_ln1w", [1, 1024])
    f_ln1b = I("f_ln1b", [1, 1024])
    f_ln2w = I("f_ln2w", [1, 4096])
    f_ln2b = I("f_ln2b", [1, 4096])
    la_wcat = I("la_wcat", [1024, 4096])
    la_qb = I("la_qb", [8, 128, 1])
    la_kb = I("la_kb", [8, 128, 1])
    la_gb = I("la_gb", [8, 128, 1])
    la_vb_row = I("la_vb_row", [1, 1024])
    la_lng = I("la_lng", [1, 1024])
    la_lnb = I("la_lnb", [1, 1024])
    triu128 = I("triu128", [128, 128])
    sw_winT = I("sw_winT", [1024, 3072])
    sw_woutT = I("sw_woutT", [1024, 1024])
    sw_tab = I("sw_tab", [2, 512, 1024])
    q_rhs = I("q_rhs", [1024, 1024])
    q_rhsT = I("q_rhsT", [1024, 1024])
    s7_qkv = I("s7_qkv", [1024, 3072])
    s7_qb = I("s7_qb", [8, 128, 1])
    s7_kb = I("s7_kb", [8, 128, 1])
    s7_vb_row = I("s7_vb_row", [1, 1024])
    s7_rq = I("s7_rq", [1024, 256])
    s7_rqb = I("s7_rqb", [2, 128, 1])
    s7_rkv = I("s7_rkv", [1024, 512])
    s7_rkvb = I("s7_rkvb", [4, 128, 1])
    s7_rout = I("s7_rout", [256, 1024])
    s7_routb_row = I("s7_routb_row", [1, 1024])
    s7_out = I("s7_out", [1024, 1024])
    s7_outb_row = I("s7_outb_row", [1, 1024])
    s7_glgr_row = I("s7_glgr_row", [1, 2048])
    loc_tab = I("loc_tab", [2, 256, 384])
    lat_iota = I("lat_iota", [128, 64])

    y = nc.dram_tensor("y", [S, D], F32, kind="ExternalOutput").ap()
    R_hbm = nc.dram_tensor("R_hbm", [S, D], F32, kind="Internal").ap()
    g_all = nc.dram_tensor("g_all", [128, 8, S], F32, kind="Internal").ap()
    c_all = nc.dram_tensor("c_all", [128, 8, S], F32, kind="Internal").ap()
    u_all = nc.dram_tensor("u_all", [128, 8, S], F32, kind="Internal").ap()
    st_all = nc.dram_tensor("st_all", [8, 128, S], BF16, kind="Internal").ap()
    w1q_hbm = nc.dram_tensor("w1q_hbm", [1024, 4096], BF16, kind="Internal").ap()
    w2q_hbm = nc.dram_tensor("w2q_hbm", [4096, 1024], BF16, kind="Internal").ap()
    sc_hbm = nc.dram_tensor("sc_hbm", [2048], F32, kind="Internal").ap()
    qrhs_hbm = nc.dram_tensor("qrhs_hbm", [1024, 1024], BF16, kind="Internal").ap()

    b = B()
    from contextlib import ExitStack
    with ExitStack() as _ctx:
        aW = _ctx.enter_context(nc.sbuf_tensor("aW", [128, 32 * 1024], BF16))
        a32 = _ctx.enter_context(nc.sbuf_tensor("a32", [128, 20 * 1024], F32))
        a16 = _ctx.enter_context(nc.sbuf_tensor("a16", [128, 24 * 1024], BF16))
        cols = _ctx.enter_context(nc.sbuf_tensor("cols", [128, 128], F32))
        onesb = _ctx.enter_context(nc.sbuf_tensor("onesb", [1, 128], BF16))
        rowsb = _ctx.enter_context(nc.sbuf_tensor("rowsb", [1, 4096], BF16))
        ones128 = _ctx.enter_context(nc.sbuf_tensor("ones128", [128, 8], BF16))
        psS = _ctx.enter_context(nc.psum_tensor("psS", [128, 2048], F32))
        psB = _ctx.enter_context(nc.psum_tensor("psB", [128, 2048], F32))
        LD = _ctx.enter_context(nc.semaphore("LD"))
        SW = _ctx.enter_context(nc.semaphore("SW"))
        TP = _ctx.enter_context(nc.semaphore("TP"))
        MM = _ctx.enter_context(nc.semaphore("MM"))
        EV = _ctx.enter_context(nc.semaphore("EV"))
        EA = _ctx.enter_context(nc.semaphore("EA"))
        PH = _ctx.enter_context(nc.semaphore("PH"))
        EG = _ctx.enter_context(nc.semaphore("EG"))
        ST = _ctx.enter_context(nc.semaphore("ST"))
        LDS = _ctx.enter_context(nc.semaphore("LDS"))
        TPa = _ctx.enter_context(nc.semaphore("TPa"))
        TPb = _ctx.enter_context(nc.semaphore("TPb"))
        block = _ctx.enter_context(nc.Block())
        def A32(col0, ncol):
            return a32[:, col0:col0 + ncol]

        def A16(col0, ncol):
            return a16[:, col0:col0 + ncol]

        def AW(col0, ncol):
            return aW[:, col0:col0 + ncol]

        def emit_ln(xt, out, F, affine=None):
            nb = F // 512
            bn = cols[:, 0:6 * nb]
            mv = cols[:, 48:50]
            sd = cols[:, 50:51]
            rstd = cols[:, 51:52]
            nmr = cols[:, 52:53]
            b.op("vector", lambda e: e.drain())
            b.op("scalar", lambda e: e.drain())
            b.wait("vector", EA)
            for i in range(nb):
                b.op("vector", lambda e, a=bn[:, 6 * i:6 * i + 6],
                     s=xt[:, 512 * i:512 * (i + 1)]: e.bn_stats(a, s))
            b.op("vector", lambda e: e.drain())
            b.op("vector", lambda e, a=mv, s=bn: e.bn_aggr(a, s), EV)
            b.wait("scalar", EV)
            b.op("scalar", lambda e, o=sd, s=mv: e.activation(
                out=o, in_=s[:, 1:2], func=AF.Sqrt, bias=epsc), EA)
            b.wait("vector", EA)
            b.op("vector", lambda e, o=rstd, s=sd: e.reciprocal(o, s))
            b.op("vector", lambda e: e.drain())
            b.op("vector", lambda e, o=nmr, s=mv, r=rstd: e.scalar_tensor_tensor(
                o, s[:, 0:1], -1.0, r, ALU.mult, ALU.mult), EV)
            b.wait("scalar", EV)
            b.op("scalar", lambda e, o=out, s=xt, r=rstd, m=nmr: e.activation(
                out=o, in_=s, func=AF.Identity, bias=m, scale=r), EA)
            if affine is not None:
                wmat, bmat = affine
                b.wait("vector", EA)
                b.op("vector", lambda e, o=out, w=wmat: e.tensor_mul(o, o, w))
                b.op("vector", lambda e: e.drain())
                b.op("vector", lambda e, o=out, bm=bmat: e.tensor_add(o, o, bm), EV)

        epsc = cols[:, 127:128]
        b.op("vector", lambda e: e.memset(epsc, EPS5), EV)
        b.op("vector", lambda e: e.drain())
        b.op("vector", lambda e: e.memset(cols[:, 120:121], 127.0), EV)
        b.op("vector", lambda e: e.drain())
        b.op("vector", lambda e: e.memset(cols[:, 121:122], 1.0), EV)
        b.op("vector", lambda e: e.drain())
        b.op("vector", lambda e: e.memset(ones128[:], 1.0), EV)
        b.dma("gpsimd", SW, onesb[:], ones_row)
        b.dma("gpsimd", SW, rowsb[0:1, 0:1024], s1_bout)
        b.wait("gpsimd", SW)
        b.wait("vector", EV)
        b.barrier(PH)

        s1on = any(p in phases for p in ("s1", "s1a", "s1b", "s1c"))
        def s1has(sub):
            return "s1" in phases or sub in phases
        if s1on:
            for k in range(8):
                b.dma("gpsimd", SW, AW(k * 1024, 1024), s1_at[k])
            for w_i, wsrc in enumerate((s1_winT, s1_wgateT, s1_bT)):
                for dk in range(8):
                    b.dma("gpsimd", SW, AW((8 + w_i * 8 + dk) * 1024, 1024),
                          wsrc[dk * 128:(dk + 1) * 128, :])
            b.dma("sync", LD, cols[:, 0:8], s1_bin.rearrange("c p one -> p (c one)"), allow_slow_non_contiguous=True)
            b.dma("sync", LD, cols[:, 8:16], s1_bgate.rearrange("c p one -> p (c one)"), allow_slow_non_contiguous=True)
            b.wait("gpsimd", SW)
            b.wait("sync", LD)
            b.barrier(PH)

            _s1a = s1has("s1a")
            for t in range(NT if _s1a else 0):
                t0 = t * TC
                xT = A32(0, 4096)
                for j in range(4):
                    b.dma("sync", LD, xT[:, j * 1024:(j + 1) * 1024],
                          xin[t0 + j * 128: t0 + (j + 1) * 128, :])
                b.wait("vector", LD)
                for j in range(4):
                    xt_j = xT[:, j * 1024:(j + 1) * 1024]
                    bn = cols[:, 16:28]
                    mv = cols[:, 28:30]
                    rstd = cols[:, 30:31]
                    nmr = cols[:, 31:32]
                    if j > 0:
                        b.op("scalar", lambda e: e.drain())
                        b.op("vector", lambda e: e.drain())
                        b.wait("vector", EA)
                    b.op("vector", lambda e, a=bn, i=xt_j: e.bn_stats(a[:, 0:6], i[:, 0:512]))
                    b.op("vector", lambda e, a=bn, i=xt_j: e.bn_stats(a[:, 6:12], i[:, 512:1024]))
                    b.op("vector", lambda e: e.drain())
                    b.op("vector", lambda e, a=mv, i=bn: e.bn_aggr(a, i), EV)
                    b.wait("scalar", EV)
                    b.op("scalar", lambda e, o=nmr, i=mv: e.activation(
                        out=o, in_=i[:, 1:2], func=AF.Sqrt, bias=epsc), EA)
                    b.wait("vector", EA)
                    b.op("vector", lambda e, o=rstd, i=nmr: e.reciprocal(o, i))
                    b.op("vector", lambda e: e.drain())
                    b.op("vector", lambda e, o=nmr, i=mv, r=rstd: e.scalar_tensor_tensor(
                        o, i[:, 0:1], -1.0, r, ALU.mult, ALU.mult), EV)
                    b.wait("scalar", EV)
                    xn_j = A16(0, 4096)[:, j * 1024:(j + 1) * 1024]
                    b.op("scalar", lambda e, o=xn_j, i=xt_j, r=rstd, m=nmr: e.activation(
                        out=o, in_=i, func=AF.Identity, bias=m, scale=r), EA)
                b.wait("scalar", EA)
                b.barrier(PH)
                xnF = A16(4096, 4096)
                for j in range(4):
                    b.dma_tr("scalar", TP,
                             xnF.rearrange("p (c n) -> p c n", c=8)[:, :, j * 128:(j + 1) * 128],
                             A16(j * 1024, 1024).rearrange("p (c n) -> p c n", c=8))
                b.wait("tensor", TP)
                inpF = A32(4096, 4096)
                inpFb = A16(8192, 4096)
                for eo in range(8):
                    bank = psB[:, (eo % 4) * 512:(eo % 4) * 512 + TC]
                    if eo >= 4:
                        b.wait("tensor", EA)
                    for dk in range(8):
                        b.op("tensor", lambda e, o=bank, l=AW((8 + 0 * 8 + dk) * 1024 + eo * 128, 128),
                             r=xnF[:, dk * TC:(dk + 1) * TC], s=(dk == 0), p=(dk == 7):
                             e.matmul(o, l, r, start=s, stop=p), MM if dk == 7 else None)
                    b.wait("scalar", MM)
                    b.op("scalar", lambda e, o=inpF[:, eo * TC:(eo + 1) * TC], i=bank,
                         bi=cols[:, eo:eo + 1]: e.activation(out=o, in_=i, func=AF.Identity, bias=bi), EA)
                    b.wait("vector", EA)
                    b.op("vector", lambda e, o=inpFb[:, eo * TC:(eo + 1) * TC],
                         i=inpF[:, eo * TC:(eo + 1) * TC]: e.tensor_copy(o, i), EV)
                b.barrier(PH)
                gF = A32(8192, 4096)
                uF = A32(12288, 4096)
                cF = A32(16384, 4096)
                b.wait("tensor", EV)
                for w_i, dstF, actf, biasoff in ((1, gF, AF.Sigmoid, 8), (2, uF, AF.Identity, None)):
                    for eo in range(8):
                        bank = psB[:, (eo % 4) * 512:(eo % 4) * 512 + TC]
                        b.wait("tensor", EA)
                        for dk in range(8):
                            b.op("tensor", lambda e, o=bank,
                                 l=AW((8 + w_i * 8 + dk) * 1024 + eo * 128, 128),
                                 r=inpFb[:, dk * TC:(dk + 1) * TC], s=(dk == 0), p=(dk == 7):
                                 e.matmul(o, l, r, start=s, stop=p), MM if dk == 7 else None)
                        b.wait("scalar", MM)
                        if w_i == 1:
                            b.op("scalar", lambda e, o=dstF[:, eo * TC:(eo + 1) * TC], i=bank,
                                 f=actf, bi=cols[:, biasoff + eo:biasoff + eo + 1]:
                                 e.activation(out=o, in_=i, func=f, bias=bi), EA)
                        else:
                            b.op("scalar", lambda e, o=dstF[:, eo * TC:(eo + 1) * TC], i=bank:
                                 e.activation(out=o, in_=i, func=AF.Identity), EA)
                b.wait("vector", EA)
                b.op("vector", lambda e, o=cF, g=gF, i=inpF: e.scalar_tensor_tensor(
                    o, g, 1.0, i, ALU.subtract, ALU.mult), EV)
                b.wait("gpsimd", EV)
                b.wait("gpsimd", EA)
                for arr, hbm in ((gF, g_all), (cF, c_all), (uF, u_all)):
                    for eo in range(8):
                        b.dma("gpsimd", SW, hbm[:, eo, t0:t0 + TC], arr[:, eo * TC:(eo + 1) * TC])
                b.wait("gpsimd", SW)
                b.barrier(PH)

            _s1b = s1has("s1b")
            ring_g = A32(0, 4096).rearrange("p (c t) -> p c t", c=8)
            ring_c = A32(4096, 4096).rearrange("p (c t) -> p c t", c=8)
            ring_u = A32(8192, 4096).rearrange("p (c t) -> p c t", c=8)
            ring_s = A16(0, 4096).rearrange("p (c t) -> p c t", c=8)
            vh = cols[:, 32:40]
            hh = cols[:, 40:48]
            gh = cols[:, 48:56]
            b.op("vector", lambda e: e.memset(A16(0, 4096), 0.0), EV)
            b.wait("tensor", EV)
            b.wait("gpsimd", EV)
            NCH = (S // CR) if _s1b else 0
            ev0 = b.cnt.get(EV.name, 0)
            ea0 = b.cnt.get(EA.name, 0)
            ld0 = b.cnt.get(LD.name, 0)
            sw0 = b.cnt.get(SW.name, 0)
            for ch in ((0, 1) if _s1b else ()):
                for hbm, ring in ((g_all, ring_g), (c_all, ring_c), (u_all, ring_u)):
                    b.dma("sync", LD, ring[:, :, (ch % 2) * CR:(ch % 2) * CR + CR],
                          hbm[:, :, ch * CR:(ch + 1) * CR])
            for ch in range(NCH):
                par = ch % 2
                b.wait("tensor", LD)
                b.wait("vector", LD)
                if ch >= 2:
                    b.wait("vector", SW)
                for ti in range(CR):
                    t = ch * CR + ti
                    tcol = par * CR + ti
                    pcol = (par * CR + ti - 1) % (2 * CR) if t > 0 else 2 * CR - 1
                    bk = 2 * (t % 2)

                    def scan_mm(e, t=t, pcol=pcol, bk=bk, ev0=ev0):
                        if t > 0:
                            e.wait_ge(EV, ev0 + 4 * t)
                        for half in range(2):
                            pb = psS[:, (bk + half) * 512:(bk + half) * 512 + 4]
                            for ei in range(4):
                                eo = half * 4 + ei
                                for k in range(8):
                                    ins = e.matmul(
                                        pb[:, ei:ei + 1],
                                        aW[:, (k * 8 + eo) * 128:(k * 8 + eo + 1) * 128],
                                        a16[:, (k * 512 + pcol):(k * 512 + pcol) + 1],
                                        start=(k == 0), stop=(k == 7))
                            ins.then_inc(MM, 1)
                    b.raw("tensor", scan_mm)
                    b.cnt[MM.name] = b.cnt.get(MM.name, 0) + 2
                    mmc = b.cnt[MM.name]

                    def scan_eltwise_v(e, t=t, tcol=tcol, bk=bk, mmc=mmc, ea0=ea0):
                        for half in range(2):
                            pb = psS[:, (bk + half) * 512:(bk + half) * 512 + 4]
                            e.wait_ge(MM, mmc - 1 + half)
                            e.tensor_add(vh[:, half * 4:half * 4 + 4], pb,
                                         ring_u[:, half * 4:half * 4 + 4, tcol]).then_inc(EV, 1)
                        for half in range(2):
                            e.wait_ge(EA, ea0 + 2 * t + half + 1)
                            e.tensor_mul(gh[:, half * 4:half * 4 + 4],
                                         ring_g[:, half * 4:half * 4 + 4, tcol],
                                         hh[:, half * 4:half * 4 + 4])
                            e.drain()
                            e.tensor_sub(ring_s[:, half * 4:half * 4 + 4, tcol],
                                         gh[:, half * 4:half * 4 + 4],
                                         ring_c[:, half * 4:half * 4 + 4, tcol]).then_inc(EV, 1)
                    b.raw("vector", scan_eltwise_v)
                    b.cnt[EV.name] = b.cnt.get(EV.name, 0) + 4
                    evc = b.cnt[EV.name]

                    def scan_tanh(e, t=t, evc=evc):
                        for half in range(2):
                            e.wait_ge(EV, evc - 4 + half + 1)
                            e.activation(out=hh[:, half * 4:half * 4 + 4],
                                         in_=vh[:, half * 4:half * 4 + 4],
                                         func=AF.Tanh).then_inc(EA, 1)
                    b.raw("scalar", scan_tanh)
                    b.cnt[EA.name] = b.cnt.get(EA.name, 0) + 2
                b.wait("gpsimd", EV)
                b.dma("gpsimd", SW, st_all.rearrange("c p s -> p c s")[:, :, ch * CR:(ch + 1) * CR],
                      ring_s[:, :, par * CR:par * CR + CR])
                if ch + 2 < NCH:
                    b.wait("sync", EV, ev0 + 4 * CR * (ch + 1))
                    for hbm, ring in ((g_all, ring_g), (c_all, ring_c), (u_all, ring_u)):
                        b.dma("sync", LD, ring[:, :, par * CR:par * CR + CR],
                              hbm[:, :, (ch + 2) * CR:(ch + 3) * CR])
            b.wait("gpsimd", SW)
            b.barrier(PH)

            for dk in range(8):
                b.dma("gpsimd", SW, AW(dk * 1024, 1024), s1_woutT[dk * 128:(dk + 1) * 128, :])
            b.wait("gpsimd", SW)
            b.barrier(PH)
            ntile = (S // 128) if s1has("s1c") else 0
            for tt in range(ntile):
                if tt > 0:
                    b.barrier(PH)
                stW = A16(0, 1024)
                xt = A32(0, 1024)
                rt = A32(1024, 1024)
                b.dma("sync", LD, stW.rearrange("p (c t) -> p c t", c=8),
                      st_all[:, :, tt * 128:(tt + 1) * 128].rearrange("c p t -> p c t"))
                b.dma("sync", LD, xt, xin[tt * 128:(tt + 1) * 128, :])
                b.wait("tensor", LD)
                for eb in range(2):
                    bank = psB[:, (eb % 2) * 512:(eb % 2) * 512 + 512]
                    if tt > 0:
                        b.wait("tensor", EV)
                    for dk in range(8):
                        b.op("tensor", lambda e, o=bank, l=stW[:, dk * 128:(dk + 1) * 128],
                             r=AW(dk * 1024 + eb * 512, 512), s=(dk == 0):
                             e.matmul(o, l, r, start=s, stop=False))
                    b.op("tensor", lambda e, o=bank, r=rowsb[0:1, eb * 512:(eb + 1) * 512]:
                         e.matmul(o, onesb[0:1, :], r, start=False, stop=True), MM)
                b.wait("vector", MM)
                for eb in range(2):
                    bank = psB[:, (eb % 2) * 512:(eb % 2) * 512 + 512]
                    b.op("vector", lambda e, o=rt[:, eb * 512:(eb + 1) * 512], i=bank,
                         x=xt[:, eb * 512:(eb + 1) * 512]: e.tensor_add(o, i, x), EV)
                b.wait("gpsimd", EV)
                b.dma("gpsimd", SW, R_hbm[tt * 128:(tt + 1) * 128, :], rt)
                b.wait("gpsimd", SW)
            b.wait("gpsimd", SW)
            b.barrier(PH)

        if "s1" not in phases:
            b.dma("gpsimd", SW, R_hbm, xin)
            b.wait("gpsimd", SW)
            b.barrier(PH)

        c127 = cols[:, 120:121]
        cone = cols[:, 121:122]

        def ffn_weight_prep():
            for wi, (wT, nrow, ncol, wq) in enumerate(
                    ((f_w1T, 1024, 4096, w1q_hbm), (f_w2T, 4096, 1024, w2q_hbm))):
                acc = cols[:, 56:57]
                tmp = cols[:, 57:58]
                b.op("vector", lambda e, a=acc: e.memset(a, 0.0), EV)
                for rc in range(nrow // 128):
                    buf = A32(0, ncol)
                    b.wait("sync", EV)
                    b.wait("sync", LDS)
                    b.dma("sync", LDS, buf, wT[rc * 128:(rc + 1) * 128, :])
                    b.wait("vector", LDS)
                    b.op("vector", lambda e, t=tmp, s=buf: e.tensor_reduce(
                        t, s, AX, ALU.add, apply_absolute_value=True))
                    b.op("vector", lambda e: e.drain())
                    b.op("vector", lambda e, a=acc, t=tmp: e.tensor_add(a, a, t))
                    b.op("vector", lambda e: e.drain())
                    b.op("vector", lambda e: e.engine_nop(), EV)
                b.wait("gpsimd", EV)
                b.dma("gpsimd", SW, sc_hbm[0:128], acc, allow_slow_non_contiguous=True)
                b.wait("gpsimd", SW)
                rowbuf = a32[0:1, 16384:16512]
                b.dma("gpsimd", SW, rowbuf, sc_hbm[0:128])
                b.wait("vector", SW)
                tot = cols[0:1, 58:59]
                invd = cols[0:1, 59:60]
                sw_ = cols[0:1, 60:61]
                b.op("vector", lambda e, o=tot, s=rowbuf: e.tensor_reduce(o, s, AX, ALU.add))
                b.op("vector", lambda e: e.drain())
                b.op("vector", lambda e, o=tot: e.tensor_scalar(
                    o, o, 2.0 ** -22, EPS5, ALU.mult, ALU.max))
                b.op("vector", lambda e: e.drain())
                b.op("vector", lambda e, o=invd, s=tot: e.tensor_scalar(
                    o, s, 0.007874015718698502, None, ALU.mult))
                b.op("vector", lambda e, o=sw_, s=tot: e.reciprocal(o, s))
                b.op("vector", lambda e: e.drain())
                b.op("vector", lambda e: e.engine_nop(), EV)
                b.wait("gpsimd", EV)
                b.dma("gpsimd", SW, sc_hbm[128 + 2 * wi:129 + 2 * wi], sw_)
                b.dma("gpsimd", SW, sc_hbm[129 + 2 * wi:130 + 2 * wi], invd)
                swb = cols[:, 61 + 2 * wi:62 + 2 * wi]
                b.wait("gpsimd", SW)
                b.dma("gpsimd", SW, swb,
                      sc_hbm[128 + 2 * wi:129 + 2 * wi].partition_broadcast(128))
                b.wait("scalar", SW)
                for rc in range(nrow // 128):
                    buf = A32(0, ncol)
                    bufq = A16(0, ncol)
                    b.wait("sync", SW)
                    b.wait("sync", EV)
                    b.wait("sync", LDS)
                    b.dma("sync", LDS, buf, wT[rc * 128:(rc + 1) * 128, :])
                    b.wait("scalar", LDS)
                    b.op("scalar", lambda e, o=buf, s=swb: e.activation(
                        out=o, in_=o, func=AF.Identity, scale=s), EA)
                    b.wait("vector", EA)
                    b.wait("vector", SW)
                    b.op("vector", lambda e, o=buf: e.tensor_scalar(
                        o, o, RNDC, -RNDC, ALU.add, ALU.add))
                    b.op("vector", lambda e: e.drain())
                    b.op("vector", lambda e, o=bufq, s=buf: e.tensor_scalar(
                        o, s, -1.0, 1.0, ALU.max, ALU.min), EV)
                    b.wait("gpsimd", EV)
                    b.dma("gpsimd", SW, wq[rc * 128:(rc + 1) * 128, :], bufq)
                b.wait("gpsimd", SW)
            b.barrier(PH)

        def ffn_consts():
            for wi in range(2):
                b.dma("gpsimd", SW, cols[:, 61 + 2 * wi:62 + 2 * wi],
                      sc_hbm[128 + 2 * wi:129 + 2 * wi].partition_broadcast(128))
                b.dma("gpsimd", SW, cols[:, 62 + 2 * wi:63 + 2 * wi],
                      sc_hbm[129 + 2 * wi:130 + 2 * wi].partition_broadcast(128))
            for off, src, width in ((8192, f_ln1w, 1024), (9216, f_ln1b, 1024),
                                    (10240, f_ln2w, 4096), (14336, f_ln2b, 4096),
                                    (18432, f_b1, 4096), (22528, f_b2, 1024)):
                b.dma("gpsimd", SW, AW(off, width),
                      src[0:1, :].squeeze(0).partition_broadcast(128))
            b.wait("gpsimd", SW)
            b.barrier(PH)

        def emit_ffn():
            M_ln1 = (AW(8192, 1024), AW(9216, 1024))
            M_ln2 = (AW(10240, 4096), AW(14336, 4096))
            M_b1 = AW(18432, 4096)
            M_b2 = AW(22528, 1024)
            NC2 = S // 256
            SLB = (0, 4096, 23552, 27648)
            SSEM = (LDS, TPa, TPb, EG)
            mm_guard = [b.cnt.get(MM.name, 0)] * 4
            lds_val = {}

            def slab_src(k):
                if k < 8:
                    return w1q_hbm[:, k * 512:(k + 1) * 512].rearrange("(c p) n -> p c n", p=128)
                eb, grp = (k - 8) // 4, (k - 8) % 4
                return (w2q_hbm[grp * 1024:(grp + 1) * 1024, eb * 512:(eb + 1) * 512]
                        .rearrange("(c p) n -> p c n", p=128))

            def slab_load(k):
                r = k % 4
                b.wait("sync", MM, mm_guard[r])
                lds_val[k] = b.dma("sync", SSEM[r],
                                   AW(SLB[r], 4096).rearrange("p (c n) -> p c n", c=8),
                                   slab_src(k))

            for ch in range(NC2):
                t0 = ch * 256
                par = ch % 2
                Rt = [A32(par * 15360 + j * 1024, 1024) for j in range(2)]
                for k in range(3):
                    slab_load(k)
                if ch == 0:
                    for j in range(2):
                        b.dma("sync", LD, Rt[j], R_hbm[t0 + j * 128:t0 + (j + 1) * 128, :])
                b.wait("vector", LD)
                b.wait("scalar", LD)
                for j in range(2):
                    xn1 = A32(12288, 1024)
                    xq32 = A32(13312, 1024)
                    emit_ln(Rt[j], xn1, 1024)
                    emit_ln(xn1, xq32, 1024, affine=M_ln1)
                    am = cols[:, 68:69]
                    scol = cols[:, 69:70]
                    c1j = cols[:, 66 + j:67 + j]
                    b.wait("vector", EV)
                    b.wait("vector", EA)
                    b.op("vector", lambda e, o=am, s=xq32: e.tensor_reduce(
                        o, s, AX, ALU.max, apply_absolute_value=True))
                    b.op("vector", lambda e: e.drain())
                    b.op("vector", lambda e, o=am: e.tensor_scalar(
                        o, o, EPS5, None, ALU.max))
                    b.op("vector", lambda e: e.drain())
                    b.op("vector", lambda e, o=c1j, a=am, i=cols[:, 62:63]:
                         e.tensor_mul(o, a, i))
                    b.op("vector", lambda e, o=scol, a=am: e.reciprocal(o, a))
                    b.op("vector", lambda e: e.drain())
                    b.op("vector", lambda e, o=scol: e.tensor_scalar(
                        o, o, 127.0, None, ALU.mult), EV)
                    b.wait("scalar", EV)
                    b.op("scalar", lambda e, o=xq32, s=scol: e.activation(
                        out=o, in_=o, func=AF.Identity, scale=s), EA)
                    b.wait("vector", EA)
                    xqstage = A16(10240, 1024)
                    b.wait("vector", TP)
                    b.op("vector", lambda e, o=xq32: e.tensor_scalar(
                        o, o, RNDC, -RNDC, ALU.add, ALU.add))
                    b.op("vector", lambda e: e.drain())
                    b.op("vector", lambda e, o=xqstage, s=xq32: e.tensor_scalar(
                        o, s, -128.0, 127.0, ALU.max, ALU.min), EV)
                    b.wait("scalar", EV)
                    b.wait("scalar", MM)
                    b.dma_tr("scalar", TP,
                             A16(j * 1024, 1024).rearrange("p (c n) -> p c n", c=8),
                             xqstage.rearrange("p (c n) -> p c n", c=8))
                if ch + 1 < NC2:
                    npar = (ch + 1) % 2
                    b.wait("sync", SW)
                    b.wait("sync", LD)
                    for j in range(2):
                        b.dma("sync", LD, A32(npar * 15360 + j * 1024, 1024),
                              R_hbm[(ch + 1) * 256 + j * 128:(ch + 1) * 256 + (j + 1) * 128, :])
                h1 = [A32(2048 + j * 4096, 4096) for j in range(2)]
                b.wait("tensor", TP)
                for hb in range(8):
                    if hb + 3 < 16:
                        slab_load(hb + 3)
                    r = hb % 4
                    slab = AW(SLB[r], 4096)
                    b.wait("tensor", SSEM[r], lds_val[hb])
                    for j in range(2):
                        bank = psB[:, ((hb % 2) * 2 + j) * 512:((hb % 2) * 2 + j) * 512 + 512]
                        b.wait("tensor", EV)
                        for dk in range(8):
                            b.op("tensor", lambda e, o=bank, l=A16(j * 1024 + dk * 128, 128),
                                 r=slab[:, dk * 512:(dk + 1) * 512], s=(dk == 0), p=(dk == 7):
                                 e.matmul(o, l, r, start=s, stop=p), MM if dk == 7 else None)
                        b.wait("vector", MM)
                        b.op("vector", lambda e, o=h1[j][:, hb * 512:(hb + 1) * 512], i=bank,
                             c=cols[:, 66 + j:67 + j], bm=M_b1[:, hb * 512:(hb + 1) * 512]:
                             e.scalar_tensor_tensor(o, i, c, bm, ALU.mult, ALU.add), EV)
                        b.wait("scalar", EV)
                        b.op("scalar", lambda e, o=h1[j][:, hb * 512:(hb + 1) * 512]:
                             e.activation(out=o, in_=o, func=AF.Gelu), EA)
                    mm_guard[r] = b.cnt[MM.name]
                for j in range(2):
                    b.wait("vector", EA)
                    emit_ln(h1[j], h1[j], 4096, affine=M_ln2)
                    am = cols[:, 68:69]
                    scol = cols[:, 69:70]
                    c2j = cols[:, 70 + j:71 + j]
                    b.wait("vector", EV)
                    b.wait("vector", EA)
                    b.op("vector", lambda e, o=am, s=h1[j]: e.tensor_reduce(
                        o, s, AX, ALU.max, apply_absolute_value=True))
                    b.op("vector", lambda e: e.drain())
                    b.op("vector", lambda e, o=am: e.tensor_scalar(
                        o, o, EPS5, None, ALU.max))
                    b.op("vector", lambda e: e.drain())
                    b.op("vector", lambda e, o=c2j, a=am, i=cols[:, 64:65]:
                         e.tensor_mul(o, a, i))
                    b.op("vector", lambda e, o=scol, a=am: e.reciprocal(o, a))
                    b.op("vector", lambda e: e.drain())
                    b.op("vector", lambda e, o=scol: e.tensor_scalar(
                        o, o, 127.0, None, ALU.mult), EV)
                    b.wait("scalar", EV)
                    b.op("scalar", lambda e, o=h1[j], s=scol: e.activation(
                        out=o, in_=o, func=AF.Identity, scale=s), EA)
                    b.wait("vector", EA)
                    hstage = A16(10240, 4096)
                    b.wait("vector", TP)
                    b.op("vector", lambda e, o=h1[j]: e.tensor_scalar(
                        o, o, RNDC, -RNDC, ALU.add, ALU.add))
                    b.op("vector", lambda e: e.drain())
                    b.op("vector", lambda e, o=hstage, s=h1[j]: e.tensor_scalar(
                        o, s, -128.0, 127.0, ALU.max, ALU.min), EV)
                    b.wait("scalar", EV)
                    b.dma_tr("scalar", TP,
                             A16(2048 + j * 4096, 4096).rearrange("p (c n) -> p c n", c=32),
                             hstage.rearrange("p (c n) -> p c n", c=32))
                b.wait("tensor", TP)
                for eb in range(2):
                    for grp in range(4):
                        k = 8 + eb * 4 + grp
                        if k + 3 < 16:
                            slab_load(k + 3)
                        r = k % 4
                        slab = AW(SLB[r], 4096)
                        b.wait("tensor", SSEM[r], lds_val[k])
                        for j in range(2):
                            bank = psB[:, (eb * 2 + j) * 512:(eb * 2 + j) * 512 + 512]
                            if grp == 0:
                                b.wait("tensor", EV)
                            for kc in range(8):
                                b.op("tensor", lambda e, o=bank,
                                     l=A16(2048 + j * 4096 + (grp * 8 + kc) * 128, 128),
                                     r=slab[:, kc * 512:(kc + 1) * 512],
                                     s=(grp == 0 and kc == 0), p=(grp == 3 and kc == 7):
                                     e.matmul(o, l, r, start=s, stop=p),
                                     MM if kc == 7 else None)
                        mm_guard[r] = b.cnt[MM.name]
                    for j in range(2):
                        bank = psB[:, (eb * 2 + j) * 512:(eb * 2 + j) * 512 + 512]
                        rtmp = A32(14336, 512)
                        b.wait("vector", MM)
                        b.op("vector", lambda e, o=rtmp, i=bank, c=cols[:, 70 + j:71 + j],
                             bm=M_b2[:, eb * 512:(eb + 1) * 512]:
                             e.scalar_tensor_tensor(o, i, c, bm, ALU.mult, ALU.add))
                        b.op("vector", lambda e: e.drain())
                        b.op("vector", lambda e, o=Rt[j][:, eb * 512:(eb + 1) * 512], t=rtmp:
                             e.tensor_add(o, o, t))
                        b.op("vector", lambda e: e.drain())
                        b.op("vector", lambda e: e.engine_nop(), EV)
                b.wait("gpsimd", EV)
                for j in range(2):
                    b.dma("gpsimd", SW, R_hbm[t0 + j * 128:t0 + (j + 1) * 128, :], Rt[j])
            b.wait("gpsimd", SW)
            b.barrier(PH)

        if "s2" in phases:
            ffn_weight_prep()
            ffn_consts()
            emit_ffn()

        def emit_s3():
            for dk in range(8):
                b.dma("gpsimd", SW, AW(dk * 4096, 4096), la_wcat[dk * 128:(dk + 1) * 128, :])
            for coff, src in ((72, la_qb), (80, la_kb), (88, la_gb)):
                b.dma("sync", LD, cols[:, coff:coff + 8],
                      src.rearrange("c p one -> p (c one)"), allow_slow_non_contiguous=True)
            b.dma("gpsimd", SW, a16[:, 12288:12416], triu128)
            for off, src in ((13952, la_lng), (14976, la_lnb)):
                b.dma("gpsimd", SW, A16(off, 1024),
                      src[0:1, :].squeeze(0).partition_broadcast(128))
            b.dma("gpsimd", SW, rowsb[0:1, 1024:2048], la_vb_row)
            S_f32 = A32(10240, 8192)
            carry = a32[0:1, 18432:19456]
            b.op("vector", lambda e: e.memset(S_f32, 0.0))
            b.op("vector", lambda e: e.memset(carry, 0.0))
            b.op("vector", lambda e: e.drain())
            b.op("vector", lambda e: e.memset(rowsb[0:1, 2048:3072], 0.0), EV)
            b.wait("gpsimd", SW)
            b.wait("sync", LD)
            b.barrier(PH)
            M3 = (A16(13952, 1024), A16(14976, 1024))
            triu = a16[:, 12288:12416]
            carry16 = rowsb[0:1, 2048:3072]
            NC3 = S // 256
            for ch in range(NC3):
                t0 = ch * 256
                Rt = [A32(j * 1024, 1024) for j in range(2)]
                for j in range(2):
                    b.dma("sync", LD, Rt[j], R_hbm[t0 + j * 128:t0 + (j + 1) * 128, :])
                b.wait("vector", LD)
                b.wait("scalar", LD)
                for j in range(2):
                    xn1 = A32(4096, 1024)
                    xn2 = A32(6144, 1024)
                    emit_ln(Rt[j], xn1, 1024)
                    emit_ln(xn1, xn2, 1024, affine=M3)
                    b.wait("vector", EV)
                    b.wait("vector", EA)
                    st16 = A16(10240, 1024)
                    b.wait("vector", TP)
                    b.op("vector", lambda e, o=st16, s=xn2: e.tensor_copy(o, s), EV)
                    b.wait("scalar", EV)
                    b.dma_tr("scalar", TP,
                             A16(0, 2048).rearrange("p (c n) -> p c n", c=8)[:, :, j * 128:(j + 1) * 128],
                             st16.rearrange("p (c n) -> p c n", c=8))
                b.wait("tensor", TP)
                xn2F = A16(0, 2048).rearrange("p (c t) -> p c t", c=8)
                def proj_F(ocbase, dstF, biascol, actf):
                    for oc in range(8):
                        bank = psB[:, (oc % 4) * 512:(oc % 4) * 512 + 256]
                        b.wait("tensor", EA)
                        for dk in range(8):
                            b.op("tensor", lambda e, o=bank,
                                 l=AW(dk * 4096 + (ocbase + oc) * 128, 128),
                                 r=a16[:, dk * 256:(dk + 1) * 256],
                                 s=(dk == 0), p=(dk == 7):
                                 e.matmul(o, l, r, start=s, stop=p), MM if dk == 7 else None)
                        b.wait("scalar", MM)
                        b.op("scalar", lambda e, o=dstF[:, oc * 256:(oc + 1) * 256], i=bank,
                             f=actf, bi=cols[:, biascol + oc:biascol + oc + 1]:
                             e.activation(out=o, in_=i, func=f, bias=bi), EA)
                gF = A32(2048, 2048)
                proj_F(24, gF, 88, AF.Sigmoid)
                qraw = A32(4096, 2048)
                proj_F(0, qraw, 72, AF.Identity)
                b.wait("vector", EA)
                wk2 = A32(6144, 2048)
                b.op("vector", lambda e, o=wk2, s=qraw: e.tensor_scalar(
                    o, s, 0.0, None, ALU.min))
                b.op("vector", lambda e: e.drain())
                b.op("vector", lambda e: e.engine_nop(), EV)
                b.wait("scalar", EV)
                b.op("scalar", lambda e, o=wk2: e.activation(out=o, in_=o, func=AF.Exp), EA)
                b.wait("vector", EA)
                b.op("vector", lambda e, o=qraw: e.tensor_scalar(o, o, 0.0, None, ALU.max))
                b.op("vector", lambda e: e.drain())
                b.op("vector", lambda e, o=A16(2048, 2048), s=qraw, t=wk2:
                     e.tensor_add(o, s, t), EV)
                kraw = A32(4096, 2048)
                b.wait("tensor", EV)
                proj_F(8, kraw, 80, AF.Identity)
                b.wait("vector", EA)
                b.op("vector", lambda e, o=kraw, g2=gF: e.tensor_mul(o, o, g2))
                b.op("vector", lambda e: e.drain())
                b.op("vector", lambda e, o=wk2, s=kraw: e.tensor_scalar(
                    o, s, 0.0, None, ALU.min))
                b.op("vector", lambda e: e.drain())
                b.op("vector", lambda e: e.engine_nop(), EV)
                b.wait("scalar", EV)
                b.op("scalar", lambda e, o=wk2: e.activation(out=o, in_=o, func=AF.Exp), EA)
                b.wait("vector", EA)
                b.op("vector", lambda e, o=kraw: e.tensor_scalar(o, o, 0.0, None, ALU.max))
                b.op("vector", lambda e: e.drain())
                b.op("vector", lambda e, o=A16(4096, 2048), s=kraw, t=wk2:
                     e.tensor_add(o, s, t), EV)
                b.wait("scalar", EV)
                for j in range(2):
                    for dk in range(8):
                        b.dma_tr("scalar", TP,
                                 a16[:, 6144 + j * 1024 + dk * 128:6144 + j * 1024 + (dk + 1) * 128],
                                 a16[:, 4096 + dk * 256 + j * 128:4096 + dk * 256 + (j + 1) * 128])
                        b.dma_tr("scalar", TP,
                                 a16[:, 10240 + j * 1024 + dk * 128:10240 + j * 1024 + (dk + 1) * 128],
                                 a16[:, 2048 + dk * 256 + j * 128:2048 + dk * 256 + (j + 1) * 128])
                b.wait("tensor", TP)
                b.wait("vector", TP)
                b.barrier(PH)
                for j in range(2):
                    for dvh in range(2):
                        bank = psB[:, (j * 2 + dvh) * 512:(j * 2 + dvh) * 512 + 512]
                        for dk in range(8):
                            b.op("tensor", lambda e, o=bank,
                                 l=a16[:, dk * 256 + j * 128:dk * 256 + (j + 1) * 128],
                                 r=AW(dk * 4096 + 2048 + dvh * 512, 512),
                                 s=(dk == 0): e.matmul(o, l, r, start=s, stop=False))
                        b.op("tensor", lambda e, o=bank,
                             r=rowsb[0:1, 1024 + dvh * 512:1024 + (dvh + 1) * 512]:
                             e.matmul(o, onesb[0:1, :], r, start=False, stop=True),
                             MM)
                b.wait("vector", MM)
                for j in range(2):
                    for dvh in range(2):
                        bank = psB[:, (j * 2 + dvh) * 512:(j * 2 + dvh) * 512 + 512]
                        b.op("vector", lambda e, o=a16[:, 8192 + j * 1024 + dvh * 512:
                                                       8192 + j * 1024 + (dvh + 1) * 512],
                             i=bank: e.tensor_copy(o, i), EV)
                b.barrier(PH)
                pts = {}
                nscore = 0
                for j in range(2):
                    for i in range(j + 1):
                        bank = psB[:, (nscore % 4) * 512:(nscore % 4) * 512 + 128]
                        for dk in range(8):
                            b.op("tensor", lambda e, o=bank,
                                 l=a16[:, 4096 + dk * 256 + i * 128:4096 + dk * 256 + (i + 1) * 128],
                                 r=a16[:, 2048 + dk * 256 + j * 128:2048 + dk * 256 + (j + 1) * 128],
                                 s=(dk == 0), p=(dk == 7):
                                 e.matmul(o, l, r, start=s, stop=p), MM if dk == 7 else None)
                        pt = a16[:, 16000 + nscore * 128:16000 + (nscore + 1) * 128]
                        pts[(j, i)] = pt
                        b.wait("vector", MM)
                        if i == j:
                            b.op("vector", lambda e, o=pt, i2=bank, t=triu:
                                 e.tensor_mul(o, i2, t), EV)
                        else:
                            b.op("vector", lambda e, o=pt, i2=bank: e.tensor_copy(o, i2), EV)
                        nscore += 1
                for j in range(2):
                    for half in range(2):
                        bank = psB[:, (j * 2 + half) * 512:(j * 2 + half) * 512 + 512]
                        b.wait("tensor", EV)
                        b.op("tensor", lambda e, o=bank, t=triu,
                             r=a16[:, 6144 + j * 1024 + half * 512:6144 + j * 1024 + (half + 1) * 512]:
                             e.matmul(o, t, r, start=True, stop=False))
                        b.op("tensor", lambda e, o=bank,
                             r=carry16[0:1, half * 512:(half + 1) * 512]:
                             e.matmul(o, onesb[0:1, :], r, start=False, stop=True), MM)
                        b.wait("vector", MM)
                        b.op("vector", lambda e, o=A32(8192 + j * 1024 + half * 512, 512),
                             i=bank: e.tensor_copy(o, i), EV)
                    for half in range(2):
                        cps = psS[0:1, half * 512:(half + 1) * 512]
                        b.op("tensor", lambda e, o=cps,
                             r=a16[:, 6144 + j * 1024 + half * 512:6144 + j * 1024 + (half + 1) * 512]:
                             e.matmul(o, ones128[:, 0:1], r, start=True, stop=True), MM)
                    b.wait("vector", MM)
                    for half in range(2):
                        cps = psS[0:1, half * 512:(half + 1) * 512]
                        b.op("vector", lambda e, o=carry[0:1, half * 512:(half + 1) * 512],
                             i=cps: e.tensor_add(o, o, i))
                    b.op("vector", lambda e: e.drain())
                    b.op("vector", lambda e, o=carry16, s=carry: e.tensor_copy(o, s), EV)
                b.barrier(PH)
                for j in range(2):
                    wk1 = A32(4096, 1024)
                    b.op("vector", lambda e, o=wk1, q=a16[:, 10240 + j * 1024:10240 + (j + 1) * 1024],
                         k2=A32(8192 + j * 1024, 1024): e.tensor_mul(o, q, k2))
                    b.op("vector", lambda e: e.drain())
                    b.op("vector", lambda e, o=cols[:, 96 + j:97 + j], s=wk1:
                         e.tensor_reduce(o, s, AX, ALU.add))
                    b.op("vector", lambda e: e.drain())
                    b.op("vector", lambda e, o=cols[:, 96 + j:97 + j]: e.tensor_scalar(
                        o, o, 1e-6, None, ALU.add))
                    b.op("vector", lambda e: e.drain())
                    b.op("vector", lambda e, o=cols[:, 98 + j:99 + j],
                         s=cols[:, 96 + j:97 + j]: e.reciprocal(o, s), EV)
                b.barrier(PH)
                mm_at_use = {}
                for dvh in range(2):
                    for dk in range(8):
                        u = dvh * 8 + dk
                        piece = A16(12416 + (u % 3) * 512, 512)
                        if u >= 3:
                            b.wait("vector", MM, mm_at_use[u - 3])
                        b.op("vector", lambda e, o=piece,
                             s=A32(10240 + dk * 1024 + dvh * 512, 512):
                             e.tensor_copy(o, s), EV)
                        b.wait("tensor", EV)
                        for j in range(2):
                            bank = psS[:, (j * 2 + dvh) * 512:(j * 2 + dvh) * 512 + 512]
                            b.op("tensor", lambda e, o=bank,
                                 l=a16[:, 2048 + dk * 256 + j * 128:2048 + dk * 256 + (j + 1) * 128],
                                 r=piece, s=(dk == 0):
                                 e.matmul(o, l, r, start=s, stop=False),
                                 MM if j == 1 else None)
                        mm_at_use[u] = b.cnt[MM.name]
                    for j in range(2):
                        bank = psS[:, (j * 2 + dvh) * 512:(j * 2 + dvh) * 512 + 512]
                        for i in range(j + 1):
                            b.op("tensor", lambda e, o=bank, l=pts[(j, i)],
                                 r=a16[:, 8192 + i * 1024 + dvh * 512:8192 + i * 1024 + (dvh + 1) * 512],
                                 p=(i == j): e.matmul(o, l, r, start=False, stop=p),
                                 MM if i == j else None)
                b.wait("vector", MM)
                for j in range(2):
                    for dvh in range(2):
                        bank = psS[:, (j * 2 + dvh) * 512:(j * 2 + dvh) * 512 + 512]
                        b.op("vector", lambda e, o=Rt[j][:, dvh * 512:(dvh + 1) * 512], i=bank,
                             dn=cols[:, 98 + j:99 + j]:
                             e.scalar_tensor_tensor(o, i, dn, o, ALU.mult, ALU.add), EV)
                b.barrier(PH)
                for dvh in range(2):
                    if dvh == 1:
                        b.wait("tensor", EV)
                    for dk in range(8):
                        bank = (psS[:, dk * 512:dk * 512 + 512] if dk < 4
                                else psB[:, (dk - 4) * 512:(dk - 4) * 512 + 512])
                        for j in range(2):
                            b.op("tensor", lambda e, o=bank,
                                 l=a16[:, 6144 + j * 1024 + dk * 128:6144 + j * 1024 + (dk + 1) * 128],
                                 r=a16[:, 8192 + j * 1024 + dvh * 512:8192 + j * 1024 + (dvh + 1) * 512],
                                 s=(j == 0), p=(j == 1):
                                 e.matmul(o, l, r, start=s, stop=p), MM if j == 1 else None)
                    b.wait("vector", MM)
                    for dk in range(8):
                        bank = (psS[:, dk * 512:dk * 512 + 512] if dk < 4
                                else psB[:, (dk - 4) * 512:(dk - 4) * 512 + 512])
                        b.op("vector", lambda e, o=A32(10240 + dk * 1024 + dvh * 512, 512),
                             i=bank: e.tensor_add(o, o, i), EV)
                b.wait("gpsimd", EV)
                for j in range(2):
                    b.dma("gpsimd", SW, R_hbm[t0 + j * 128:t0 + (j + 1) * 128, :], Rt[j])
                b.wait("gpsimd", SW)
                b.barrier(PH)

        if "s3" in phases:
            emit_s3()

        if "s4" in phases:
            if "s2" not in phases:
                ffn_weight_prep()
            ffn_consts()
            emit_ffn()

        def emit_s5():
            for dk in range(8):
                b.dma("gpsimd", SW, AW(dk * 3072, 3072), sw_winT[dk * 128:(dk + 1) * 128, :])
                b.dma("gpsimd", SW, AW(24576 + dk * 1024, 1024), sw_woutT[dk * 128:(dk + 1) * 128, :])
            b.op("vector", lambda e: e.memset(a16[:, 8192:24576], 0.0), EV)
            b.wait("gpsimd", SW)
            b.barrier(PH)
            slopes = [float(np.exp2(np.float32(-(h + 1) * 8.0 / 16.0))) for h in range(16)]
            NC5 = S // 512
            for ch in range(NC5):
                t0 = ch * 512
                par, prv = ch % 2, 1 - ch % 2
                Rt = [A32(j * 1024, 1024) for j in range(4)]
                for j in range(4):
                    b.dma("sync", LD, Rt[j], R_hbm[t0 + j * 128:t0 + (j + 1) * 128, :])
                if ch <= 1:
                    for j in range(4):
                        b.dma("sync", LD, A32(4096 + j * 1024, 1024),
                              sw_tab[min(ch, 1)][j * 128:(j + 1) * 128, :])
                b.wait("vector", LD)
                b.wait("scalar", LD)
                for j in range(4):
                    xn1 = A32(8192, 1024)
                    emit_ln(Rt[j], xn1, 1024)
                    st16 = A16(4096, 1024)
                    b.wait("vector", EA)
                    b.wait("vector", TP)
                    b.op("vector", lambda e, o=st16, s=xn1: e.tensor_copy(o, s), EV)
                    b.wait("scalar", EV)
                    b.dma_tr("scalar", TP,
                             A16(0, 4096).rearrange("p (c n) -> p c n", c=8)[:, :, j * 128:(j + 1) * 128],
                             st16.rearrange("p (c n) -> p c n", c=8))
                b.wait("tensor", TP)
                for oc in range(16):
                    bank = psB[:, (oc % 4) * 512:(oc % 4) * 512 + 512]
                    b.wait("tensor", EA)
                    for dk in range(8):
                        b.op("tensor", lambda e, o=bank,
                             l=AW(dk * 3072 + oc * 128, 128),
                             r=a16[:, dk * 512:(dk + 1) * 512],
                             s=(dk == 0), p=(dk == 7):
                             e.matmul(o, l, r, start=s, stop=p), MM if dk == 7 else None)
                    b.wait("scalar", MM)
                    if oc < 8:
                        b.op("scalar", lambda e, o=a16[:, 4096 + oc * 512:4096 + (oc + 1) * 512],
                             i=bank: e.activation(out=o, in_=i, func=AF.Identity, scale=0.125), EA)
                    else:
                        b.op("scalar", lambda e, o=a16[:, 8192 + par * 4096 + (oc - 8) * 512:
                                                       8192 + par * 4096 + (oc - 7) * 512],
                             i=bank: e.activation(out=o, in_=i, func=AF.Identity), EA)
                for j in range(4):
                    for vh in range(2):
                        bank = psS[:, ((j * 2 + vh) % 4) * 512:((j * 2 + vh) % 4) * 512 + 512]
                        b.wait("tensor", EV)
                        for dk in range(8):
                            b.op("tensor", lambda e, o=bank,
                                 l=a16[:, dk * 512 + j * 128:dk * 512 + (j + 1) * 128],
                                 r=AW(dk * 3072 + 2048 + vh * 512, 512),
                                 s=(dk == 0), p=(dk == 7):
                                 e.matmul(o, l, r, start=s, stop=p), MM if dk == 7 else None)
                        b.wait("vector", MM)
                        b.op("vector", lambda e, o=a16[:, 16384 + par * 4096 + j * 1024 + vh * 512:
                                                       16384 + par * 4096 + j * 1024 + (vh + 1) * 512],
                             i=bank: e.tensor_copy(o, i), EV)
                b.barrier(PH)
                for j in range(4):
                    outbank = [psS[:, 0:512], psS[:, 512:1024]]
                    mm_sc, mm_pv, ev_stt, tp_h, ea_h = {}, {}, {}, {}, {}

                    def s5_scores(h):
                        p = h % 2
                        if h >= 2:
                            b.wait("tensor", EV, ev_stt[h - 2])
                        for half, kpar in ((0, prv), (1, par)):
                            bank = psB[:, p * 1024 + half * 512:p * 1024 + (half + 1) * 512]
                            b.op("tensor", lambda e, o=bank,
                                 l=a16[64 * (h % 2):64 * (h % 2) + 64,
                                       4096 + (h // 2) * 512 + j * 128:4096 + (h // 2) * 512 + (j + 1) * 128],
                                 r=a16[64 * (h % 2):64 * (h % 2) + 64,
                                       8192 + kpar * 4096 + (h // 2) * 512:8192 + kpar * 4096 + (h // 2 + 1) * 512]:
                                 e.matmul(o, l, r, start=True, stop=True), MM)
                        mm_sc[h] = b.cnt[MM.name]
                        p16 = A16(p * 1024, 1024)
                        s32 = A32(8192 + p * 1024, 1024)
                        nm = cols[:, 104:105]
                        rs = cols[:, 105:106]
                        b.wait("vector", MM, mm_sc[h])
                        for half in range(2):
                            bank = psB[:, p * 1024 + half * 512:p * 1024 + (half + 1) * 512]
                            b.op("vector", lambda e, o=s32[:, half * 512:(half + 1) * 512],
                                 t=A32(4096 + j * 1024 + half * 512, 512), sl=slopes[h], i=bank:
                                 e.scalar_tensor_tensor(o, t, sl, i, ALU.mult, ALU.add), EV)
                        ev_stt[h] = b.cnt[EV.name]
                        b.op("vector", lambda e: e.drain())
                        b.op("vector", lambda e, o=nm, s=s32: e.tensor_reduce(
                            o, s, AX, ALU.max, negate=True), EV)
                        b.wait("scalar", EV)
                        if h >= 2:
                            b.wait("scalar", TPa if p == 0 else TPb, tp_h[h - 2])
                        b.op("scalar", lambda e, o=p16, s=s32, m=nm: e.activation(
                            out=o, in_=s, func=AF.Exp, bias=m), EA)
                        ea_h[h] = b.cnt[EA.name]
                        b.wait("vector", EA, ea_h[h])
                        b.op("vector", lambda e, o=rs, s=p16: e.tensor_reduce(o, s, AX, ALU.add))
                        b.op("vector", lambda e: e.drain())
                        b.op("vector", lambda e, o=rs: e.reciprocal(o, o))
                        b.op("vector", lambda e: e.drain())
                        if h >= 2:
                            b.wait("vector", TPa if p == 0 else TPb, tp_h[h - 2])
                        b.op("vector", lambda e, o=p16, r=rs: e.tensor_scalar(
                            o, o, r, None, ALU.mult), EV)
                        b.wait("scalar", EV)
                        if h >= 2:
                            b.wait("scalar", MM, mm_pv[h - 2])
                            b.wait("scalar", TPa if p == 0 else TPb, tp_h[h - 2])
                        b.dma_tr("scalar", TPa if p == 0 else TPb,
                                 A16(2048 + p * 1024, 1024).rearrange("p (c n) -> p c n", c=8),
                                 p16.rearrange("p (c n) -> p c n", c=8))
                        tp_h[h] = b.cnt[(TPa if p == 0 else TPb).name]

                    def s5_pv(h):
                        p = h % 2
                        b.wait("tensor", TPa if p == 0 else TPb, tp_h[h])
                        for kt in range(8):
                            kpar = prv if kt < 4 else par
                            vt = kt % 4
                            b.op("tensor", lambda e, o=outbank[h // 8][:, (h % 8) * 64:(h % 8) * 64 + 64],
                                 l=A16(2048 + p * 1024 + kt * 128, 128),
                                 r=a16[:, 16384 + kpar * 4096 + vt * 1024 + h * 64:
                                       16384 + kpar * 4096 + vt * 1024 + (h + 1) * 64],
                                 s=(kt == 0), p2=(kt == 7):
                                 e.matmul(o, l, r, start=s, stop=p2), MM if kt == 7 else None)
                        mm_pv[h] = b.cnt[MM.name]

                    s5_scores(0)
                    for h in range(1, 16):
                        s5_scores(h)
                        s5_pv(h - 1)
                    s5_pv(15)
                    b.wait("vector", MM)
                    b.wait("vector", TPa)
                    b.wait("vector", TPb)
                    o16 = A16(0, 1024)
                    for hb in range(2):
                        b.op("vector", lambda e, o=o16[:, hb * 512:(hb + 1) * 512],
                             i=outbank[hb]: e.tensor_copy(o, i), EV)
                    b.wait("scalar", EV)
                    b.wait("scalar", MM)
                    b.dma_tr("scalar", TP,
                             A16(1024, 1024).rearrange("p (c n) -> p c n", c=8),
                             o16.rearrange("p (c n) -> p c n", c=8))
                    b.wait("tensor", TP)
                    for eb in range(2):
                        bank = psS[:, 1024 + eb * 512:1024 + eb * 512 + 512]
                        b.wait("tensor", EV)
                        for dk in range(8):
                            b.op("tensor", lambda e, o=bank, l=A16(1024 + dk * 128, 128),
                                 r=AW(24576 + dk * 1024 + eb * 512, 512),
                                 s=(dk == 0), p=(dk == 7):
                                 e.matmul(o, l, r, start=s, stop=p), MM if dk == 7 else None)
                        b.wait("vector", MM)
                        b.op("vector", lambda e, o=Rt[j][:, eb * 512:(eb + 1) * 512],
                             i=bank: e.tensor_add(o, o, i), EV)
                b.wait("gpsimd", EV)
                for j in range(4):
                    b.dma("gpsimd", SW, R_hbm[t0 + j * 128:t0 + (j + 1) * 128, :], Rt[j])
                b.wait("gpsimd", SW)
                b.barrier(PH)

        if "s5" in phases:
            emit_s5()

        def s6_prep():
            for rc in range(8):
                buf = A32(8192, 1024)
                am = cols[:, 106:107]
                b.wait("sync", EV)
                b.wait("sync", LDS)
                b.dma("sync", LDS, buf, q_rhsT[rc * 128:(rc + 1) * 128, :])
                b.wait("vector", LDS)
                b.wait("vector", SW)
                b.op("vector", lambda e, o=am, s=buf: e.tensor_reduce(
                    o, s, AX, ALU.max, apply_absolute_value=True))
                b.op("vector", lambda e: e.drain())
                b.op("vector", lambda e: e.engine_nop(), EV)
                b.wait("gpsimd", EV)
                b.dma("gpsimd", SW, sc_hbm[256 + rc * 128:256 + (rc + 1) * 128], am,
                      allow_slow_non_contiguous=True)
            b.wait("gpsimd", SW)
            rowf = a32[0:1, 8192:9216]
            b.dma("gpsimd", SW, rowf, sc_hbm[256:1280])
            b.wait("vector", SW)
            b.op("vector", lambda e, o=rowf: e.tensor_scalar(
                o, o, 1e-12, 0.007874015718698502, ALU.max, ALU.mult))
            b.op("vector", lambda e: e.drain())
            b.op("vector", lambda e: e.engine_nop(), EV)
            b.wait("gpsimd", EV)
            b.dma("gpsimd", SW, sc_hbm[1024:2048], rowf)
            b.wait("gpsimd", SW)
            rsmat = A32(4096, 1024)
            b.dma("gpsimd", SW, rsmat, sc_hbm[1024:2048].partition_broadcast(128))
            b.wait("sync", SW)
            b.wait("vector", SW)
            rimat = A32(5120, 1024)
            b.op("vector", lambda e, o=rimat, s=rsmat: e.reciprocal(o, s))
            b.op("vector", lambda e: e.drain())
            for rc in range(8):
                buf = A32(8192, 1024)
                bufq = A16(0, 1024)
                b.wait("sync", SW)
                b.wait("sync", EV)
                b.wait("sync", LDS)
                b.dma("sync", LDS, buf, q_rhs[rc * 128:(rc + 1) * 128, :])
                b.wait("vector", LDS)
                b.op("vector", lambda e, o=buf, r=rimat: e.tensor_tensor(o, o, r, ALU.mult))
                b.op("vector", lambda e: e.drain())
                b.op("vector", lambda e, o=buf: e.tensor_scalar(
                    o, o, RNDC, -RNDC, ALU.add, ALU.add))
                b.op("vector", lambda e: e.drain())
                b.op("vector", lambda e, o=bufq, s=buf: e.tensor_scalar(
                    o, s, -127.0, 127.0, ALU.max, ALU.min), EV)
                b.wait("gpsimd", EV)
                b.dma("gpsimd", SW, qrhs_hbm[rc * 128:(rc + 1) * 128, :], bufq)
            b.wait("gpsimd", SW)
            b.barrier(PH)

        def emit_s6():
            for rc in range(8):
                b.dma("gpsimd", SW, AW(rc * 1024, 1024), qrhs_hbm[rc * 128:(rc + 1) * 128, :])
            rsmat = A32(4096, 1024)
            b.dma("gpsimd", SW, rsmat, sc_hbm[1024:2048].partition_broadcast(128))
            b.wait("gpsimd", SW)
            b.barrier(PH)
            NC6 = S // 512
            for ch in range(NC6):
                t0 = ch * 512
                Rt = [A32(j * 1024, 1024) for j in range(4)]
                for j in range(4):
                    b.dma("sync", LD, Rt[j], R_hbm[t0 + j * 128:t0 + (j + 1) * 128, :])
                b.wait("vector", LD)
                b.wait("scalar", LD)
                for j in range(4):
                    xn = A32(5120, 1024)
                    xq = A32(6144, 1024)
                    emit_ln(Rt[j], xn, 1024)
                    am = cols[:, 106:107]
                    lsj = cols[:, 107 + j:108 + j]
                    b.wait("vector", EA)
                    b.op("vector", lambda e, o=am, s=xn: e.tensor_reduce(
                        o, s, AX, ALU.max, apply_absolute_value=True))
                    b.op("vector", lambda e: e.drain())
                    b.op("vector", lambda e, o=lsj, a=am: e.tensor_scalar(
                        o, a, 1e-12, 0.007874015718698502, ALU.max, ALU.mult))
                    b.op("vector", lambda e: e.drain())
                    b.op("vector", lambda e, o=cols[:, 112:113], s=lsj: e.reciprocal(o, s))
                    b.op("vector", lambda e: e.drain())
                    b.op("vector", lambda e, o=xq, s=xn, l=cols[:, 112:113]: e.tensor_scalar(
                        o, s, l, None, ALU.mult))
                    b.op("vector", lambda e: e.drain())
                    b.op("vector", lambda e, o=xq: e.tensor_scalar(
                        o, o, RNDC, -RNDC, ALU.add, ALU.add))
                    b.op("vector", lambda e: e.drain())
                    st16 = A16(1024, 1024)
                    b.wait("vector", TP)
                    b.op("vector", lambda e, o=st16, s=xq: e.tensor_scalar(
                        o, s, -127.0, 127.0, ALU.max, ALU.min), EV)
                    b.wait("scalar", EV)
                    b.dma_tr("scalar", TP,
                             A16(2048 + j * 1024, 1024).rearrange("p (c n) -> p c n", c=8),
                             st16.rearrange("p (c n) -> p c n", c=8))
                b.wait("tensor", TP)
                for j in range(4):
                    for eb in range(2):
                        bank = (psB if j % 2 == 0 else psS)[:, eb * 512:eb * 512 + 512]
                        b.wait("tensor", EV)
                        for dk in range(8):
                            b.op("tensor", lambda e, o=bank,
                                 l=A16(2048 + j * 1024 + dk * 128, 128),
                                 r=AW(dk * 1024 + eb * 512, 512),
                                 s=(dk == 0), p=(dk == 7):
                                 e.matmul(o, l, r, start=s, stop=p), MM if dk == 7 else None)
                        b.wait("vector", MM)
                        tmp = A32(7168, 512)
                        b.op("vector", lambda e, o=tmp, i=bank, l=cols[:, 107 + j:108 + j],
                             r=rsmat[:, eb * 512:(eb + 1) * 512]:
                             e.scalar_tensor_tensor(o, i, l, r, ALU.mult, ALU.mult))
                        b.op("vector", lambda e: e.drain())
                        b.op("vector", lambda e, o=Rt[j][:, eb * 512:(eb + 1) * 512], t=tmp:
                             e.tensor_add(o, o, t), EV)
                b.wait("gpsimd", EV)
                for j in range(4):
                    b.dma("gpsimd", SW, R_hbm[t0 + j * 128:t0 + (j + 1) * 128, :], Rt[j])
                b.wait("gpsimd", SW)
                b.barrier(PH)

        if "s6" in phases:
            s6_prep()
            emit_s6()

        def emit_s7():
            for dk in range(8):
                b.dma("gpsimd", SW, AW(dk * 3072, 3072), s7_qkv[dk * 128:(dk + 1) * 128, :])
                b.dma("gpsimd", SW, AW(24576 + dk * 256, 256), s7_rq[dk * 128:(dk + 1) * 128, :])
                b.dma("gpsimd", SW, AW(26624 + dk * 512, 512), s7_rkv[dk * 128:(dk + 1) * 128, :])
            for ck in range(2):
                b.dma("gpsimd", SW, AW(30720 + ck * 1024, 1024), s7_rout[ck * 128:(ck + 1) * 128, :])
            for dk in range(8):
                b.dma("gpsimd", SW, A16(16384 + dk * 1024, 1024), s7_out[dk * 128:(dk + 1) * 128, :])
            b.dma("gpsimd", SW, rowsb[0:1, 0:1024], s7_vb_row)
            b.dma("gpsimd", SW, rowsb[0:1, 1024:2048], s7_routb_row)
            b.dma("gpsimd", SW, rowsb[0:1, 2048:3072], s7_outb_row)
            b.dma("sync", LD, cols[:, 72:80], s7_qb.rearrange("c p one -> p (c one)"),
                  allow_slow_non_contiguous=True)
            b.dma("sync", LD, cols[:, 80:88], s7_kb.rearrange("c p one -> p (c one)"),
                  allow_slow_non_contiguous=True)
            b.dma("sync", LD, cols[:, 88:90], s7_rqb.rearrange("c p one -> p (c one)"),
                  allow_slow_non_contiguous=True)
            b.dma("sync", LD, cols[:, 90:94], s7_rkvb.rearrange("c p one -> p (c one)"),
                  allow_slow_non_contiguous=True)
            b.dma("sync", LD, A32(2816, 64), lat_iota)
            b.wait("gpsimd", SW)
            b.wait("vector", LD)
            b.op("vector", lambda e, o=cols[:, 72:80]: e.tensor_scalar(
                o, o, 0.125, None, ALU.mult))
            b.op("vector", lambda e, o=cols[:, 88:90]: e.tensor_scalar(
                o, o, 0.125, None, ALU.mult))
            b.op("vector", lambda e: e.drain())
            b.op("vector", lambda e: e.memset(a16[:, 4096:12288], 0.0))
            b.op("vector", lambda e: e.drain())
            b.op("vector", lambda e: e.memset(a16[:, 14080:15248], 0.0))
            b.op("vector", lambda e: e.drain())
            b.op("vector", lambda e: e.engine_nop(), EV)
            b.wait("gpsimd", EV)
            b.dma("gpsimd", SW, A32(3904, 1024),
                  s7_glgr_row[0:1, 0:1024].squeeze(0).partition_broadcast(128))
            b.dma("gpsimd", SW, A32(4928, 1024),
                  s7_glgr_row[0:1, 1024:2048].squeeze(0).partition_broadcast(128))
            b.wait("gpsimd", SW)
            b.barrier(PH)
            glmat, grmat = A32(3904, 1024), A32(4928, 1024)
            NC7 = S // 256
            for ch in range(NC7):
                t0 = ch * 256
                par, prv = ch % 2, 1 - ch % 2
                kFo = [4096, 6144]
                vTo = [8192, 10240]
                Rt = [A32(j * 1024, 1024) for j in range(2)]
                for j in range(2):
                    b.dma("sync", LD, Rt[j], R_hbm[t0 + j * 128:t0 + (j + 1) * 128, :])
                if ch <= 1:
                    for j in range(2):
                        b.dma("sync", LD, A32(2048 + j * 384, 384),
                              loc_tab[min(ch, 1)][j * 128:(j + 1) * 128, :])
                b.wait("vector", LD)
                b.wait("scalar", LD)
                for j in range(2):
                    xn1 = A32(2880, 1024)
                    emit_ln(Rt[j], xn1, 1024)
                    st16 = A16(12288, 1024)
                    b.wait("vector", EA)
                    b.wait("vector", TP)
                    b.op("vector", lambda e, o=st16, s=xn1: e.tensor_copy(o, s), EV)
                    b.wait("scalar", EV)
                    b.dma_tr("scalar", TP,
                             A16(0, 2048).rearrange("p (c n) -> p c n", c=8)[:, :, j * 128:(j + 1) * 128],
                             st16.rearrange("p (c n) -> p c n", c=8))
                b.wait("tensor", TP)
                b.wait("vector", TP)
                pooled32 = A32(7504, 16)
                for dk in range(8):
                    for j in range(2):
                        b.op("vector", lambda e, o=pooled32[:, dk * 2 + j:dk * 2 + j + 1],
                             s=a16[:, dk * 256 + j * 128:dk * 256 + (j + 1) * 128]:
                             e.tensor_reduce(o, s, AX, ALU.add))
                b.op("vector", lambda e: e.drain())
                b.op("vector", lambda e, o=pooled32: e.tensor_scalar(
                    o, o, 2.0 ** -7, None, ALU.mult))
                b.op("vector", lambda e: e.drain())
                p16 = A16(15232, 16)
                b.op("vector", lambda e, o=p16, s=pooled32: e.tensor_copy(o, s), EV)
                b.wait("tensor", EV)
                for oc in range(4):
                    bank = psB[:, 0:2]
                    if oc > 0:
                        b.wait("tensor", EA)
                    for dk in range(8):
                        b.op("tensor", lambda e, o=bank,
                             l=AW(26624 + dk * 512 + oc * 128, 128),
                             r=p16[:, dk * 2:dk * 2 + 2], s=(dk == 0), p=(dk == 7):
                             e.matmul(o, l, r, start=s, stop=p), MM if dk == 7 else None)
                    b.wait("scalar", MM)
                    if oc < 2:
                        dst = a16[:, 14080 + oc * 64 + 2 * ch:14080 + oc * 64 + 2 * ch + 2]
                    else:
                        dst = a16[:, 14208 + (oc - 2) * 128 + 2 * ch:14208 + (oc - 2) * 128 + 2 * ch + 2]
                    b.op("scalar", lambda e, o=dst, i=bank, bi=cols[:, 90 + oc:91 + oc]:
                         e.activation(out=o, in_=i, func=AF.Identity, bias=bi), EA)
                def projF7(wbase, ocbase, noc, dsto, biascol, scale=None):
                    for oc in range(noc):
                        bank = psB[:, (oc % 4) * 512:(oc % 4) * 512 + 256]
                        b.wait("tensor", EA)
                        for dk in range(8):
                            b.op("tensor", lambda e, o=bank,
                                 l=AW(wbase + dk * (3072 if wbase == 0 else 256) + (ocbase + oc) * 128, 128),
                                 r=a16[:, dk * 256:(dk + 1) * 256],
                                 s=(dk == 0), p=(dk == 7):
                                 e.matmul(o, l, r, start=s, stop=p), MM if dk == 7 else None)
                        b.wait("scalar", MM)
                        if scale is None:
                            b.op("scalar", lambda e, o=a16[:, dsto + oc * 256:dsto + (oc + 1) * 256],
                                 i=bank, bi=cols[:, biascol + oc:biascol + oc + 1]:
                                 e.activation(out=o, in_=i, func=AF.Identity, bias=bi), EA)
                        else:
                            b.op("scalar", lambda e, o=a16[:, dsto + oc * 256:dsto + (oc + 1) * 256],
                                 i=bank, bi=cols[:, biascol + oc:biascol + oc + 1], sc=scale:
                                 e.activation(out=o, in_=i, func=AF.Identity, bias=bi, scale=sc), EA)
                projF7(0, 0, 8, 2048, 72, 0.125)
                projF7(0, 8, 8, kFo[par], 80)
                projF7(24576, 0, 2, 14720, 88, 0.125)
                for j in range(2):
                    for vh in range(2):
                        bank = psS[:, (j * 2 + vh) * 512:(j * 2 + vh) * 512 + 512]
                        b.wait("tensor", EV)
                        for dk in range(8):
                            b.op("tensor", lambda e, o=bank,
                                 l=a16[:, dk * 256 + j * 128:dk * 256 + (j + 1) * 128],
                                 r=AW(dk * 3072 + 2048 + vh * 512, 512),
                                 s=(dk == 0): e.matmul(o, l, r, start=s, stop=False))
                        b.op("tensor", lambda e, o=bank,
                             r=rowsb[0:1, vh * 512:(vh + 1) * 512]:
                             e.matmul(o, onesb[0:1, :], r, start=False, stop=True), MM)
                        b.wait("vector", MM)
                        b.op("vector", lambda e, o=a16[:, vTo[par] + j * 1024 + vh * 512:
                                                       vTo[par] + j * 1024 + (vh + 1) * 512],
                             i=bank: e.tensor_copy(o, i), EV)
                b.wait("scalar", EA)
                b.wait("scalar", EV)
                for ck in range(2):
                    b.dma_tr("scalar", TP, A16(14464 + ck * 128, 128),
                             A16(14208 + ck * 128, 128))
                b.wait("tensor", TP)
                b.wait("vector", TP)
                b.barrier(PH)
                for j in range(2):
                    cid = 2 * ch + j
                    locbank = psS[:, (2 * j) * 512:(2 * j) * 512 + 1024]
                    mm_sc, mm_pv, ev_stt, tp_h = {}, {}, {}, {}
                    P7off = [13696, 15248]
                    PToff = [13312, 15632]

                    def s7_scores(h):
                        p = h % 2
                        hp = 64 * (h % 2)
                        scb = psB[:, 1024 + p * 512:1024 + p * 512 + 384]
                        if h >= 2:
                            b.wait("tensor", EV, ev_stt[h - 2])
                        srcs = [(kFo[prv] + (h // 2) * 256 + 128, prv),
                                (kFo[par] + (h // 2) * 256, par),
                                (kFo[par] + (h // 2) * 256 + 128, par)]
                        for kt in range(3):
                            b.op("tensor", lambda e, o=scb[:, kt * 128:(kt + 1) * 128],
                                 l=a16[hp:hp + 64, 2048 + (h // 2) * 256 + j * 128:
                                       2048 + (h // 2) * 256 + (j + 1) * 128],
                                 r=a16[hp:hp + 64, srcs[kt][0]:srcs[kt][0] + 128]:
                                 e.matmul(o, l, r, start=True, stop=True),
                                 MM if kt == 2 else None)
                        mm_sc[h] = b.cnt[MM.name]
                        s32 = A32(7056 + p * 384, 384)
                        nm = cols[:, 104:105]
                        rs = cols[:, 105:106]
                        P7 = A16(P7off[p], 384)
                        b.wait("vector", MM, mm_sc[h])
                        b.op("vector", lambda e, o=s32, i=scb, t=A32(2048 + j * 384, 384):
                             e.tensor_add(o, i, t), EV)
                        ev_stt[h] = b.cnt[EV.name]
                        b.op("vector", lambda e: e.drain())
                        b.op("vector", lambda e, o=nm, s=s32: e.tensor_reduce(
                            o, s, AX, ALU.max, negate=True), EV)
                        b.wait("scalar", EV)
                        if h >= 2:
                            b.wait("scalar", TPa if p == 0 else TPb, tp_h[h - 2])
                        b.op("scalar", lambda e, o=P7, s=s32, m=nm: e.activation(
                            out=o, in_=s, func=AF.Exp, bias=m), EA)
                        b.wait("vector", EA)
                        b.op("vector", lambda e, o=rs, s=P7: e.tensor_reduce(o, s, AX, ALU.add))
                        b.op("vector", lambda e: e.drain())
                        b.op("vector", lambda e, o=rs: e.reciprocal(o, o))
                        b.op("vector", lambda e: e.drain())
                        if h >= 2:
                            b.wait("vector", TPa if p == 0 else TPb, tp_h[h - 2])
                        b.op("vector", lambda e, o=P7, r=rs: e.tensor_scalar(
                            o, o, r, None, ALU.mult), EV)
                        if ch == 0 and j == 0:
                            b.op("vector", lambda e: e.drain())
                            b.op("vector", lambda e, o=P7[0:1, :]: e.tensor_scalar(
                                o, o, 0.0, None, ALU.mult), EV)
                        b.wait("scalar", EV)
                        if h >= 2:
                            b.wait("scalar", MM, mm_pv[h - 2])
                            b.wait("scalar", TPa if p == 0 else TPb, tp_h[h - 2])
                        b.dma_tr("scalar", TPa if p == 0 else TPb,
                                 A16(PToff[p], 384).rearrange("p (c n) -> p c n", c=3),
                                 P7.rearrange("p (c n) -> p c n", c=3))
                        tp_h[h] = b.cnt[(TPa if p == 0 else TPb).name]

                    def s7_pv(h):
                        p = h % 2
                        b.wait("tensor", TPa if p == 0 else TPb, tp_h[h])
                        vsrcs = [vTo[prv] + 1024, vTo[par], vTo[par] + 1024]
                        for kt in range(3):
                            b.op("tensor", lambda e,
                                 o=locbank[:, h * 64:(h + 1) * 64],
                                 l=A16(PToff[p] + kt * 128, 128),
                                 r=a16[:, vsrcs[kt] + h * 64:vsrcs[kt] + (h + 1) * 64],
                                 s=(kt == 0), p2=(kt == 2):
                                 e.matmul(o, l, r, start=s, stop=p2), MM if kt == 2 else None)
                        mm_pv[h] = b.cnt[MM.name]

                    s7_scores(0)
                    for h in range(1, 16):
                        s7_scores(h)
                        s7_pv(h - 1)
                    s7_pv(15)
                    latbank = psS[:, (2 * (1 - j)) * 512:(2 * (1 - j)) * 512 + 256]
                    lsb = psB[:, 1024:1088]
                    b.wait("tensor", EV)
                    b.wait("tensor", TPa)
                    b.wait("tensor", TPb)
                    if cid > 0:
                        cmpm = A32(6992, 64)
                        b.wait("vector", MM)
                        b.op("vector", lambda e, o=cmpm, s=A32(2816, 64), c=float(cid):
                             e.tensor_scalar(o, s, c, -1e30, ALU.is_ge, ALU.mult), EV)
                        b.op("vector", lambda e: e.drain())
                        b.op("vector", lambda e, o=A16(13696 + 64, 64): e.memset(o, 0.0), EV)
                        for lh in range(4):
                            lhp = 64 * (lh % 2)
                            b.wait("tensor", EV)
                            b.op("tensor", lambda e, o=lsb,
                                 l=a16[lhp:lhp + 64, 14720 + (lh // 2) * 256 + j * 128:
                                       14720 + (lh // 2) * 256 + (j + 1) * 128],
                                 r=a16[lhp:lhp + 64, 14080 + (lh // 2) * 64:
                                       14080 + (lh // 2) * 64 + 64]:
                                 e.matmul(o, l, r, start=True, stop=True), MM)
                            b.wait("vector", MM)
                            s32l = A32(7440, 64)
                            b.op("vector", lambda e, o=s32l, i=lsb, t=cmpm:
                                 e.tensor_add(o, i, t))
                            b.op("vector", lambda e: e.drain())
                            nm = cols[:, 104:105]
                            rs = cols[:, 105:106]
                            b.op("vector", lambda e, o=nm, s=s32l: e.tensor_reduce(
                                o, s, AX, ALU.max, negate=True), EV)
                            b.wait("scalar", EV)
                            b.wait("scalar", TP)
                            Plat = A16(13696, 64)
                            b.op("scalar", lambda e, o=Plat, s=s32l, m=nm: e.activation(
                                out=o, in_=s, func=AF.Exp, bias=m), EA)
                            b.wait("vector", EA)
                            b.op("vector", lambda e, o=rs, s=Plat: e.tensor_reduce(
                                o, s, AX, ALU.add))
                            b.op("vector", lambda e: e.drain())
                            b.op("vector", lambda e, o=rs: e.reciprocal(o, o))
                            b.op("vector", lambda e: e.drain())
                            b.wait("vector", TP)
                            b.op("vector", lambda e, o=Plat, r=rs: e.tensor_scalar(
                                o, o, r, None, ALU.mult), EV)
                            b.wait("scalar", EV)
                            b.wait("scalar", MM)
                            b.dma_tr("scalar", TP, A16(13312, 128), A16(13696, 128))
                            b.wait("tensor", TP)
                            b.op("tensor", lambda e, o=latbank[:, lh * 64:(lh + 1) * 64],
                                 l=A16(13312, 128),
                                 r=A16(14464 + (lh // 2) * 128 + (lh % 2) * 64, 64):
                                 e.matmul(o, l, r, start=True, stop=True), MM)
                    if cid > 0:
                        lat16 = A16(13312, 256)
                        b.wait("vector", MM)
                        b.wait("vector", TP)
                        b.op("vector", lambda e, o=lat16, i=latbank: e.tensor_copy(o, i), EV)
                        b.wait("scalar", EV)
                        b.wait("scalar", MM)
                        b.dma_tr("scalar", TP,
                                 A16(13696, 256).rearrange("p (c n) -> p c n", c=2),
                                 lat16.rearrange("p (c n) -> p c n", c=2))
                        b.wait("tensor", TP)
                    for eb in range(2):
                        rbank = psB[:, eb * 512:eb * 512 + 512]
                        b.wait("tensor", EV)
                        b.wait("tensor", EA)
                        if cid > 0:
                            for ck in range(2):
                                b.op("tensor", lambda e, o=rbank, l=A16(13696 + ck * 128, 128),
                                     r=AW(30720 + ck * 1024 + eb * 512, 512),
                                     s=(ck == 0): e.matmul(o, l, r, start=s, stop=False))
                        b.op("tensor", lambda e, o=rbank,
                             r=rowsb[0:1, 1024 + eb * 512:1024 + (eb + 1) * 512],
                             s=(cid == 0): e.matmul(o, onesb[0:1, :], r, start=s, stop=True),
                             MM)
                    b.wait("vector", MM)
                    lmul = A32(5952, 1024)
                    rmt32 = A32(7552, 1024)
                    gate16 = A32(6976, 16)
                    b.op("vector", lambda e, o=rmt32, i=psB[:, 0:1024]: e.tensor_copy(o, i))
                    b.op("vector", lambda e: e.drain())
                    b.op("vector", lambda e, o=lmul, i=locbank, g2=glmat:
                         e.tensor_mul(o, i, g2))
                    b.op("vector", lambda e: e.drain())
                    b.op("vector", lambda e, o=gate16, s=lmul.rearrange("p (h d) -> p h d", h=16):
                         e.tensor_reduce(o, s, AX, ALU.add))
                    b.op("vector", lambda e: e.drain())
                    b.op("vector", lambda e, o=lmul, i=rmt32, g2=grmat:
                         e.tensor_mul(o, i, g2))
                    b.op("vector", lambda e: e.drain())
                    b.op("vector", lambda e, o=A32(7056, 16),
                         s=lmul.rearrange("p (h d) -> p h d", h=16):
                         e.tensor_reduce(o, s, AX, ALU.add))
                    b.op("vector", lambda e: e.drain())
                    b.op("vector", lambda e, o=gate16, t=A32(7056, 16):
                         e.tensor_add(o, o, t))
                    b.op("vector", lambda e: e.drain())
                    b.op("vector", lambda e: e.engine_nop(), EV)
                    b.wait("scalar", EV)
                    b.op("scalar", lambda e, o=gate16: e.activation(
                        out=o, in_=o, func=AF.Sigmoid), EA)
                    b.wait("vector", EA)
                    b.wait("vector", TP)
                    mx16 = A16(12288, 1024)
                    t1 = A32(6992, 64)
                    for h in range(16):
                        b.op("vector", lambda e, o=t1, i=locbank[:, h * 64:(h + 1) * 64],
                             r=rmt32[:, h * 64:(h + 1) * 64]: e.tensor_sub(o, i, r))
                        b.op("vector", lambda e: e.drain())
                        b.op("vector", lambda e, o=mx16[:, h * 64:(h + 1) * 64], i=t1,
                             g2=gate16[:, h:h + 1], r=rmt32[:, h * 64:(h + 1) * 64]:
                             e.scalar_tensor_tensor(o, i, g2, r, ALU.mult, ALU.add))
                        b.op("vector", lambda e: e.drain())
                    b.op("vector", lambda e: e.engine_nop(), EV)
                    b.wait("scalar", EV)
                    b.wait("scalar", MM)
                    b.dma_tr("scalar", TP,
                             A16(15248, 1024).rearrange("p (c n) -> p c n", c=8),
                             mx16.rearrange("p (c n) -> p c n", c=8))
                    b.wait("tensor", TP)
                    for eb in range(2):
                        bank = psB[:, eb * 512:(eb + 1) * 512]
                        b.wait("tensor", EV)
                        for dk in range(8):
                            b.op("tensor", lambda e, o=bank, l=A16(15248 + dk * 128, 128),
                                 r=A16(16384 + dk * 1024 + eb * 512, 512),
                                 s=(dk == 0): e.matmul(o, l, r, start=s, stop=False))
                        b.op("tensor", lambda e, o=bank,
                             r=rowsb[0:1, 2048 + eb * 512:2048 + (eb + 1) * 512]:
                             e.matmul(o, onesb[0:1, :], r, start=False, stop=True), MM)
                        b.wait("vector", MM)
                        b.op("vector", lambda e, o=Rt[j][:, eb * 512:(eb + 1) * 512],
                             i=bank: e.tensor_add(o, o, i), EV)
                b.wait("gpsimd", EV)
                for j in range(2):
                    b.dma("gpsimd", SW, R_hbm[t0 + j * 128:t0 + (j + 1) * 128, :], Rt[j])
                b.wait("gpsimd", SW)
                b.barrier(PH)

        if "s7" in phases:
            emit_s7()

        if "s8" in phases:
            if "s6" not in phases:
                s6_prep()
            emit_s6()

        b.dma("gpsimd", ST, y, R_hbm)
        b.wait("gpsimd", ST)

        b.replay(block)
    return nc


def kernel(**inputs):
    import os
    S = int(np.asarray(inputs["L0_attn_a_x"]).shape[1])
    ph = tuple(p for p in os.environ.get("BASS_PHASES", ",".join(ALL_PHASES)).split(",") if p)
    nc = build_nc(S=S, phases=ph)
    consts = host_consts(S)
    in_maps = []
    for c in range(8):
        m = host_layout(inputs, lane=c % 4)
        m.update(consts)
        in_maps.append(m)
    res = run_bass_kernel_spmd(nc, in_maps, core_ids=list(range(8)))
    out = np.stack([res.results[c]["y"] for c in range(4)], axis=0)
    return out.astype(np.float32)


# version 11
# speedup vs baseline: 1.1923x; 1.1923x over previous
import os as _os
_os.environ.setdefault("GAUGE_NTFF_IGNORE_DMA", "1")
import numpy as np
import ml_dtypes
import concourse.bass as bass
import concourse.mybir as mybir
from concourse.bass_utils import run_bass_kernel_spmd

F32 = mybir.dt.float32
BF16 = mybir.dt.bfloat16
AF = mybir.ActivationFunctionType
ALU = mybir.AluOpType
AX = mybir.AxisListType.X

D = 1024
TC = 512
CR = 256
RNDC = 12582912.0
NEGBIG = -1e30
EPS5 = 1e-5

ENGS = ("sync", "scalar", "vector", "tensor", "gpsimd")
ALL_PHASES = ("s1", "s2", "s3", "s4", "s5", "s6", "s7", "s8")


def host_layout(inputs, lane):
    g = lambda k: np.ascontiguousarray(np.asarray(inputs[k]))
    d = {}
    d["xin"] = np.ascontiguousarray(np.asarray(inputs["L0_attn_a_x"])[lane])
    A = np.asarray(inputs["L0_attn_a_A"])
    at = np.empty((8, 128, 1024), np.float32)
    for k in range(8):
        at[k] = A[:, k * 128:(k + 1) * 128].T
    d["s1_at"] = np.ascontiguousarray(at)
    d["s1_winT"] = g("L0_attn_a_W_in").T.copy()
    d["s1_wgateT"] = g("L0_attn_a_W_gate").T.copy()
    d["s1_bT"] = g("L0_attn_a_B").T.copy()
    d["s1_woutT"] = g("L0_attn_a_W_out").T.copy()
    d["s1_bin"] = g("L0_attn_a_b_in").reshape(8, 128, 1).copy()
    d["s1_bgate"] = g("L0_attn_a_b_gate").reshape(8, 128, 1).copy()
    d["s1_bout"] = g("L0_attn_a_b_out").reshape(1, 1024).copy()
    d["f_w1T"] = g("L0_ffn_w1").T.copy()
    d["f_w2T"] = g("L0_ffn_w2").T.copy()
    d["f_b1"] = g("L0_ffn_b1").reshape(1, 4096).copy()
    d["f_b2"] = g("L0_ffn_b2").reshape(1, 1024).copy()
    d["f_ln1w"] = g("L0_ffn_ln1_w").reshape(1, 1024).copy()
    d["f_ln1b"] = g("L0_ffn_ln1_b").reshape(1, 1024).copy()
    d["f_ln2w"] = g("L0_ffn_ln2_w").reshape(1, 4096).copy()
    d["f_ln2b"] = g("L0_ffn_ln2_b").reshape(1, 4096).copy()
    qkv_w = g("L0_attn_b_qkv_w")
    gate_w = g("L0_attn_b_gate_w")
    d["la_wcat"] = np.concatenate([qkv_w, gate_w], axis=1).copy()
    qkv_b = g("L0_attn_b_qkv_b")
    d["la_qb"] = qkv_b[0:1024].reshape(8, 128, 1).copy()
    d["la_kb"] = qkv_b[1024:2048].reshape(8, 128, 1).copy()
    d["la_vb_row"] = qkv_b[2048:3072].reshape(1, 1024).copy()
    d["la_gb"] = g("L0_attn_b_gate_b").reshape(8, 128, 1).copy()
    d["la_lng"] = g("L0_attn_b_ln_g").reshape(1, 1024).copy()
    d["la_lnb"] = g("L0_attn_b_ln_b").reshape(1, 1024).copy()
    d["sw_winT"] = g("L1_attn_a_w_in").T.copy()
    d["sw_woutT"] = g("L1_attn_a_w_out").T.copy()
    d["q_rhs"] = g("L1_ffn_rhs").copy()
    d["q_rhsT"] = g("L1_ffn_rhs").T.copy()
    d["s7_qkv"] = g("L1_attn_b_qkv_w").copy()
    qb7 = g("L1_attn_b_qkv_b")
    d["s7_qb"] = qb7[0:1024].reshape(8, 128, 1).copy()
    d["s7_kb"] = qb7[1024:2048].reshape(8, 128, 1).copy()
    d["s7_vb_row"] = qb7[2048:3072].reshape(1, 1024).copy()
    d["s7_rq"] = g("L1_attn_b_rq_w").copy()
    d["s7_rqb"] = g("L1_attn_b_rq_b").reshape(2, 128, 1).copy()
    d["s7_rkv"] = g("L1_attn_b_rkv_w").copy()
    d["s7_rkvb"] = g("L1_attn_b_rkv_b").reshape(4, 128, 1).copy()
    d["s7_rout"] = g("L1_attn_b_rout_w").copy()
    d["s7_routb_row"] = g("L1_attn_b_rout_b").reshape(1, 1024).copy()
    d["s7_out"] = g("L1_attn_b_out_w").copy()
    d["s7_outb_row"] = g("L1_attn_b_out_b").reshape(1, 1024).copy()
    gl = g("L1_attn_b_gl_w").reshape(64)
    gr = g("L1_attn_b_gr_w").reshape(64)
    d["s7_glgr_row"] = np.concatenate([np.tile(gl, 16), np.tile(gr, 16)]).reshape(1, 2048).copy()
    return d


def host_consts(S):
    c = {}
    c["ones_row"] = np.ones((1, 128), np.float32)
    c["triu128"] = np.triu(np.ones((128, 128), np.float32))
    WIN = 512
    qi = np.arange(WIN)[:, None]
    kj = np.arange(2 * WIN)[None, :]
    delta = (qi - kj + WIN).astype(np.float32)
    valid = (delta >= 0) & (delta <= WIN)
    tab1 = np.where(valid, delta, NEGBIG).astype(np.float32)
    tab0 = np.where(valid & (kj >= WIN), delta, NEGBIG).astype(np.float32)
    c["sw_tab"] = np.ascontiguousarray(np.stack([tab0, tab1]))
    ql = np.arange(256)[:, None]
    idx = np.arange(384)[None, :]
    dist = ql - idx + 128
    allowed = (dist > 0) & (dist <= 128)
    t0_ = np.where(allowed & (idx >= 128), 0.0, NEGBIG).astype(np.float32)
    t1_ = np.where(allowed, 0.0, NEGBIG).astype(np.float32)
    c["loc_tab"] = np.ascontiguousarray(np.stack([t0_, t1_]))
    c["lat_iota"] = np.ascontiguousarray(
        np.broadcast_to(np.arange(64, dtype=np.float32)[None, :], (128, 64)).copy())
    return c


class B:

    def __init__(self):
        self.q = {e: [] for e in ENGS}
        self.cnt = {}
        self.nbar = 0

    def raw(self, eng, fn):
        self.q[eng].append(fn)

    def op(self, eng, fn, sem=None, n=1):
        if sem is None:
            self.q[eng].append(fn)
            return self.cnt
        self.cnt[sem.name] = self.cnt.get(sem.name, 0) + n
        def run(e, fn=fn, sem=sem, n=n):
            fn(e).then_inc(sem, n)
        self.q[eng].append(run)
        return self.cnt[sem.name]

    def dma(self, eng, sem, out, in_, **kw):
        return self.op(eng, lambda e: e.dma_start(out=out, in_=in_, **kw), sem, 16)

    def dma_tr(self, eng, sem, out, in_):
        return self.op(eng, lambda e: e.dma_start_transpose(out, in_), sem, 16)

    def wait(self, eng, sem, val=None):
        v = self.cnt.get(sem.name, 0) if val is None else val
        self.q[eng].append(lambda e, sem=sem, v=v: e.wait_ge(sem, v))
        return v

    def barrier(self, PH, engs=ENGS):
        self.nbar += 1
        tgt = self.nbar * len(engs)
        for eng in engs:
            self.q[eng].append(lambda e: e.drain())
            self.q[eng].append(lambda e, PH=PH: e.sem_inc(PH, 1))
        for eng in engs:
            self.q[eng].append(lambda e, PH=PH, tgt=tgt: e.wait_ge(PH, tgt))
        self.cnt[PH.name] = tgt

    def replay(self, block):
        for eng in ENGS:
            getattr(block, eng)(lambda e, eng=eng: [f(e) for f in self.q[eng]])


def build_nc(S=8192, phases=ALL_PHASES):
    NT = S // TC
    nc = bass.Bass("TRN2", target_bir_lowering=False)

    def I(name, shape, dtype=F32):
        return nc.dram_tensor(name, list(shape), dtype, kind="ExternalInput").ap()

    xin = I("xin", [S, D])
    s1_at = I("s1_at", [8, 128, 1024])
    s1_winT = I("s1_winT", [D, D])
    s1_wgateT = I("s1_wgateT", [D, D])
    s1_bT = I("s1_bT", [D, D])
    s1_woutT = I("s1_woutT", [D, D])
    s1_bin = I("s1_bin", [8, 128, 1])
    s1_bgate = I("s1_bgate", [8, 128, 1])
    s1_bout = I("s1_bout", [1, D])
    ones_row = I("ones_row", [1, 128])
    f_w1T = I("f_w1T", [1024, 4096])
    f_w2T = I("f_w2T", [4096, 1024])
    f_b1 = I("f_b1", [1, 4096])
    f_b2 = I("f_b2", [1, 1024])
    f_ln1w = I("f_ln1w", [1, 1024])
    f_ln1b = I("f_ln1b", [1, 1024])
    f_ln2w = I("f_ln2w", [1, 4096])
    f_ln2b = I("f_ln2b", [1, 4096])
    la_wcat = I("la_wcat", [1024, 4096])
    la_qb = I("la_qb", [8, 128, 1])
    la_kb = I("la_kb", [8, 128, 1])
    la_gb = I("la_gb", [8, 128, 1])
    la_vb_row = I("la_vb_row", [1, 1024])
    la_lng = I("la_lng", [1, 1024])
    la_lnb = I("la_lnb", [1, 1024])
    triu128 = I("triu128", [128, 128])
    sw_winT = I("sw_winT", [1024, 3072])
    sw_woutT = I("sw_woutT", [1024, 1024])
    sw_tab = I("sw_tab", [2, 512, 1024])
    q_rhs = I("q_rhs", [1024, 1024])
    q_rhsT = I("q_rhsT", [1024, 1024])
    s7_qkv = I("s7_qkv", [1024, 3072])
    s7_qb = I("s7_qb", [8, 128, 1])
    s7_kb = I("s7_kb", [8, 128, 1])
    s7_vb_row = I("s7_vb_row", [1, 1024])
    s7_rq = I("s7_rq", [1024, 256])
    s7_rqb = I("s7_rqb", [2, 128, 1])
    s7_rkv = I("s7_rkv", [1024, 512])
    s7_rkvb = I("s7_rkvb", [4, 128, 1])
    s7_rout = I("s7_rout", [256, 1024])
    s7_routb_row = I("s7_routb_row", [1, 1024])
    s7_out = I("s7_out", [1024, 1024])
    s7_outb_row = I("s7_outb_row", [1, 1024])
    s7_glgr_row = I("s7_glgr_row", [1, 2048])
    loc_tab = I("loc_tab", [2, 256, 384])
    lat_iota = I("lat_iota", [128, 64])

    y = nc.dram_tensor("y", [S, D], F32, kind="ExternalOutput").ap()
    R_hbm = nc.dram_tensor("R_hbm", [S, D], F32, kind="Internal").ap()
    g_all = nc.dram_tensor("g_all", [128, 8, S], F32, kind="Internal").ap()
    c_all = nc.dram_tensor("c_all", [128, 8, S], F32, kind="Internal").ap()
    u_all = nc.dram_tensor("u_all", [128, 8, S], F32, kind="Internal").ap()
    st_all = nc.dram_tensor("st_all", [8, 128, S], BF16, kind="Internal").ap()
    w1q_hbm = nc.dram_tensor("w1q_hbm", [1024, 4096], BF16, kind="Internal").ap()
    w2q_hbm = nc.dram_tensor("w2q_hbm", [4096, 1024], BF16, kind="Internal").ap()
    sc_hbm = nc.dram_tensor("sc_hbm", [2048], F32, kind="Internal").ap()
    qrhs_hbm = nc.dram_tensor("qrhs_hbm", [1024, 1024], BF16, kind="Internal").ap()

    b = B()
    from contextlib import ExitStack
    with ExitStack() as _ctx:
        aW = _ctx.enter_context(nc.sbuf_tensor("aW", [128, 32 * 1024], BF16))
        a32 = _ctx.enter_context(nc.sbuf_tensor("a32", [128, 20 * 1024], F32))
        a16 = _ctx.enter_context(nc.sbuf_tensor("a16", [128, 24 * 1024], BF16))
        cols = _ctx.enter_context(nc.sbuf_tensor("cols", [128, 128], F32))
        onesb = _ctx.enter_context(nc.sbuf_tensor("onesb", [1, 128], BF16))
        rowsb = _ctx.enter_context(nc.sbuf_tensor("rowsb", [1, 4096], BF16))
        ones128 = _ctx.enter_context(nc.sbuf_tensor("ones128", [128, 8], BF16))
        psS = _ctx.enter_context(nc.psum_tensor("psS", [128, 2048], F32))
        psB = _ctx.enter_context(nc.psum_tensor("psB", [128, 2048], F32))
        LD = _ctx.enter_context(nc.semaphore("LD"))
        SW = _ctx.enter_context(nc.semaphore("SW"))
        TP = _ctx.enter_context(nc.semaphore("TP"))
        MM = _ctx.enter_context(nc.semaphore("MM"))
        EV = _ctx.enter_context(nc.semaphore("EV"))
        EA = _ctx.enter_context(nc.semaphore("EA"))
        PH = _ctx.enter_context(nc.semaphore("PH"))
        EG = _ctx.enter_context(nc.semaphore("EG"))
        ST = _ctx.enter_context(nc.semaphore("ST"))
        LDS = _ctx.enter_context(nc.semaphore("LDS"))
        TPa = _ctx.enter_context(nc.semaphore("TPa"))
        TPb = _ctx.enter_context(nc.semaphore("TPb"))
        block = _ctx.enter_context(nc.Block())
        def A32(col0, ncol):
            return a32[:, col0:col0 + ncol]

        def A16(col0, ncol):
            return a16[:, col0:col0 + ncol]

        def AW(col0, ncol):
            return aW[:, col0:col0 + ncol]

        def emit_ln(xt, out, F, affine=None):
            nb = F // 512
            bn = cols[:, 0:6 * nb]
            mv = cols[:, 48:50]
            sd = cols[:, 50:51]
            rstd = cols[:, 51:52]
            nmr = cols[:, 52:53]
            b.op("vector", lambda e: e.drain())
            b.op("scalar", lambda e: e.drain())
            b.wait("vector", EA)
            for i in range(nb):
                b.op("vector", lambda e, a=bn[:, 6 * i:6 * i + 6],
                     s=xt[:, 512 * i:512 * (i + 1)]: e.bn_stats(a, s))
            b.op("vector", lambda e: e.drain())
            b.op("vector", lambda e, a=mv, s=bn: e.bn_aggr(a, s), EV)
            b.wait("scalar", EV)
            b.op("scalar", lambda e, o=sd, s=mv: e.activation(
                out=o, in_=s[:, 1:2], func=AF.Sqrt, bias=epsc), EA)
            b.wait("vector", EA)
            b.op("vector", lambda e, o=rstd, s=sd: e.reciprocal(o, s))
            b.op("vector", lambda e: e.drain())
            b.op("vector", lambda e, o=nmr, s=mv, r=rstd: e.scalar_tensor_tensor(
                o, s[:, 0:1], -1.0, r, ALU.mult, ALU.mult), EV)
            b.wait("scalar", EV)
            b.op("scalar", lambda e, o=out, s=xt, r=rstd, m=nmr: e.activation(
                out=o, in_=s, func=AF.Identity, bias=m, scale=r), EA)
            if affine is not None:
                wmat, bmat = affine
                b.wait("vector", EA)
                b.op("vector", lambda e, o=out, w=wmat: e.tensor_mul(o, o, w))
                b.op("vector", lambda e: e.drain())
                b.op("vector", lambda e, o=out, bm=bmat: e.tensor_add(o, o, bm), EV)

        epsc = cols[:, 127:128]
        b.op("vector", lambda e: e.memset(epsc, EPS5), EV)
        b.op("vector", lambda e: e.drain())
        b.op("vector", lambda e: e.memset(cols[:, 120:121], 127.0), EV)
        b.op("vector", lambda e: e.drain())
        b.op("vector", lambda e: e.memset(cols[:, 121:122], 1.0), EV)
        b.op("vector", lambda e: e.drain())
        b.op("vector", lambda e: e.memset(ones128[:], 1.0), EV)
        b.dma("gpsimd", SW, onesb[:], ones_row)
        b.dma("gpsimd", SW, rowsb[0:1, 0:1024], s1_bout)
        b.wait("gpsimd", SW)
        b.wait("vector", EV)
        b.barrier(PH)

        s1on = any(p in phases for p in ("s1", "s1a", "s1b", "s1c"))
        def s1has(sub):
            return "s1" in phases or sub in phases
        if s1on:
            for k in range(8):
                b.dma("gpsimd", SW, AW(k * 1024, 1024), s1_at[k])
            for w_i, wsrc in enumerate((s1_winT, s1_wgateT, s1_bT)):
                for dk in range(8):
                    b.dma("gpsimd", SW, AW((8 + w_i * 8 + dk) * 1024, 1024),
                          wsrc[dk * 128:(dk + 1) * 128, :])
            b.dma("sync", LD, cols[:, 0:8], s1_bin.rearrange("c p one -> p (c one)"), allow_slow_non_contiguous=True)
            b.dma("sync", LD, cols[:, 8:16], s1_bgate.rearrange("c p one -> p (c one)"), allow_slow_non_contiguous=True)
            b.wait("gpsimd", SW)
            b.wait("sync", LD)
            b.barrier(PH)

            _s1a = s1has("s1a")
            for t in range(NT if _s1a else 0):
                t0 = t * TC
                xT = A32(0, 4096)
                for j in range(4):
                    b.dma("sync", LD, xT[:, j * 1024:(j + 1) * 1024],
                          xin[t0 + j * 128: t0 + (j + 1) * 128, :])
                b.wait("vector", LD)
                for j in range(4):
                    xt_j = xT[:, j * 1024:(j + 1) * 1024]
                    bn = cols[:, 16:28]
                    mv = cols[:, 28:30]
                    rstd = cols[:, 30:31]
                    nmr = cols[:, 31:32]
                    if j > 0:
                        b.op("scalar", lambda e: e.drain())
                        b.op("vector", lambda e: e.drain())
                        b.wait("vector", EA)
                    b.op("vector", lambda e, a=bn, i=xt_j: e.bn_stats(a[:, 0:6], i[:, 0:512]))
                    b.op("vector", lambda e, a=bn, i=xt_j: e.bn_stats(a[:, 6:12], i[:, 512:1024]))
                    b.op("vector", lambda e: e.drain())
                    b.op("vector", lambda e, a=mv, i=bn: e.bn_aggr(a, i), EV)
                    b.wait("scalar", EV)
                    b.op("scalar", lambda e, o=nmr, i=mv: e.activation(
                        out=o, in_=i[:, 1:2], func=AF.Sqrt, bias=epsc), EA)
                    b.wait("vector", EA)
                    b.op("vector", lambda e, o=rstd, i=nmr: e.reciprocal(o, i))
                    b.op("vector", lambda e: e.drain())
                    b.op("vector", lambda e, o=nmr, i=mv, r=rstd: e.scalar_tensor_tensor(
                        o, i[:, 0:1], -1.0, r, ALU.mult, ALU.mult), EV)
                    b.wait("scalar", EV)
                    xn_j = A16(0, 4096)[:, j * 1024:(j + 1) * 1024]
                    b.op("scalar", lambda e, o=xn_j, i=xt_j, r=rstd, m=nmr: e.activation(
                        out=o, in_=i, func=AF.Identity, bias=m, scale=r), EA)
                b.wait("scalar", EA)
                b.barrier(PH)
                xnF = A16(4096, 4096)
                for j in range(4):
                    b.dma_tr("scalar", TP,
                             xnF.rearrange("p (c n) -> p c n", c=8)[:, :, j * 128:(j + 1) * 128],
                             A16(j * 1024, 1024).rearrange("p (c n) -> p c n", c=8))
                b.wait("tensor", TP)
                inpF = A32(4096, 4096)
                inpFb = A16(8192, 4096)
                for eo in range(8):
                    bank = psB[:, (eo % 4) * 512:(eo % 4) * 512 + TC]
                    if eo >= 4:
                        b.wait("tensor", EA)
                    for dk in range(8):
                        b.op("tensor", lambda e, o=bank, l=AW((8 + 0 * 8 + dk) * 1024 + eo * 128, 128),
                             r=xnF[:, dk * TC:(dk + 1) * TC], s=(dk == 0), p=(dk == 7):
                             e.matmul(o, l, r, start=s, stop=p), MM if dk == 7 else None)
                    b.wait("scalar", MM)
                    b.op("scalar", lambda e, o=inpF[:, eo * TC:(eo + 1) * TC], i=bank,
                         bi=cols[:, eo:eo + 1]: e.activation(out=o, in_=i, func=AF.Identity, bias=bi), EA)
                    b.wait("vector", EA)
                    b.op("vector", lambda e, o=inpFb[:, eo * TC:(eo + 1) * TC],
                         i=inpF[:, eo * TC:(eo + 1) * TC]: e.tensor_copy(o, i), EV)
                b.barrier(PH)
                gF = A32(8192, 4096)
                uF = A32(12288, 4096)
                cF = A32(16384, 4096)
                b.wait("tensor", EV)
                for w_i, dstF, actf, biasoff in ((1, gF, AF.Sigmoid, 8), (2, uF, AF.Identity, None)):
                    for eo in range(8):
                        bank = psB[:, (eo % 4) * 512:(eo % 4) * 512 + TC]
                        b.wait("tensor", EA)
                        for dk in range(8):
                            b.op("tensor", lambda e, o=bank,
                                 l=AW((8 + w_i * 8 + dk) * 1024 + eo * 128, 128),
                                 r=inpFb[:, dk * TC:(dk + 1) * TC], s=(dk == 0), p=(dk == 7):
                                 e.matmul(o, l, r, start=s, stop=p), MM if dk == 7 else None)
                        b.wait("scalar", MM)
                        if w_i == 1:
                            b.op("scalar", lambda e, o=dstF[:, eo * TC:(eo + 1) * TC], i=bank,
                                 f=actf, bi=cols[:, biasoff + eo:biasoff + eo + 1]:
                                 e.activation(out=o, in_=i, func=f, bias=bi), EA)
                        else:
                            b.op("scalar", lambda e, o=dstF[:, eo * TC:(eo + 1) * TC], i=bank:
                                 e.activation(out=o, in_=i, func=AF.Identity), EA)
                b.wait("vector", EA)
                b.op("vector", lambda e, o=cF, g=gF, i=inpF: e.scalar_tensor_tensor(
                    o, g, 1.0, i, ALU.subtract, ALU.mult), EV)
                b.wait("gpsimd", EV)
                b.wait("gpsimd", EA)
                for arr, hbm in ((gF, g_all), (cF, c_all), (uF, u_all)):
                    for eo in range(8):
                        b.dma("gpsimd", SW, hbm[:, eo, t0:t0 + TC], arr[:, eo * TC:(eo + 1) * TC])
                b.wait("gpsimd", SW)
                b.barrier(PH)

            _s1b = s1has("s1b")
            ring_g = A32(0, 4096).rearrange("p (c t) -> p c t", c=8)
            ring_c = A32(4096, 4096).rearrange("p (c t) -> p c t", c=8)
            ring_u = A32(8192, 4096).rearrange("p (c t) -> p c t", c=8)
            ring_s = A16(0, 4096).rearrange("p (c t) -> p c t", c=8)
            vh = cols[:, 32:40]
            hh = cols[:, 40:48]
            gh = cols[:, 48:56]
            b.op("vector", lambda e: e.memset(A16(0, 4096), 0.0), EV)
            b.wait("tensor", EV)
            b.wait("gpsimd", EV)
            NCH = (S // CR) if _s1b else 0
            ev0 = b.cnt.get(EV.name, 0)
            ea0 = b.cnt.get(EA.name, 0)
            ld0 = b.cnt.get(LD.name, 0)
            sw0 = b.cnt.get(SW.name, 0)
            for ch in ((0, 1) if _s1b else ()):
                for hbm, ring in ((g_all, ring_g), (c_all, ring_c), (u_all, ring_u)):
                    b.dma("sync", LD, ring[:, :, (ch % 2) * CR:(ch % 2) * CR + CR],
                          hbm[:, :, ch * CR:(ch + 1) * CR])
            for ch in range(NCH):
                par = ch % 2
                b.wait("tensor", LD)
                b.wait("vector", LD)
                if ch >= 2:
                    b.wait("vector", SW)
                for ti in range(CR):
                    t = ch * CR + ti
                    tcol = par * CR + ti
                    pcol = (par * CR + ti - 1) % (2 * CR) if t > 0 else 2 * CR - 1
                    bk = 2 * (t % 2)

                    def scan_mm(e, t=t, pcol=pcol, bk=bk, ev0=ev0):
                        if t > 0:
                            e.wait_ge(EV, ev0 + 4 * t)
                        for half in range(2):
                            pb = psS[:, (bk + half) * 512:(bk + half) * 512 + 4]
                            for ei in range(4):
                                eo = half * 4 + ei
                                for k in range(8):
                                    ins = e.matmul(
                                        pb[:, ei:ei + 1],
                                        aW[:, (k * 8 + eo) * 128:(k * 8 + eo + 1) * 128],
                                        a16[:, (k * 512 + pcol):(k * 512 + pcol) + 1],
                                        start=(k == 0), stop=(k == 7))
                            ins.then_inc(MM, 1)
                    b.raw("tensor", scan_mm)
                    b.cnt[MM.name] = b.cnt.get(MM.name, 0) + 2
                    mmc = b.cnt[MM.name]

                    def scan_eltwise_v(e, t=t, tcol=tcol, bk=bk, mmc=mmc, ea0=ea0):
                        for half in range(2):
                            pb = psS[:, (bk + half) * 512:(bk + half) * 512 + 4]
                            e.wait_ge(MM, mmc - 1 + half)
                            e.tensor_add(vh[:, half * 4:half * 4 + 4], pb,
                                         ring_u[:, half * 4:half * 4 + 4, tcol]).then_inc(EV, 1)
                        for half in range(2):
                            e.wait_ge(EA, ea0 + 2 * t + half + 1)
                            e.tensor_mul(gh[:, half * 4:half * 4 + 4],
                                         ring_g[:, half * 4:half * 4 + 4, tcol],
                                         hh[:, half * 4:half * 4 + 4])
                            e.drain()
                            e.tensor_sub(ring_s[:, half * 4:half * 4 + 4, tcol],
                                         gh[:, half * 4:half * 4 + 4],
                                         ring_c[:, half * 4:half * 4 + 4, tcol]).then_inc(EV, 1)
                    b.raw("vector", scan_eltwise_v)
                    b.cnt[EV.name] = b.cnt.get(EV.name, 0) + 4
                    evc = b.cnt[EV.name]

                    def scan_tanh(e, t=t, evc=evc):
                        for half in range(2):
                            e.wait_ge(EV, evc - 4 + half + 1)
                            e.activation(out=hh[:, half * 4:half * 4 + 4],
                                         in_=vh[:, half * 4:half * 4 + 4],
                                         func=AF.Tanh).then_inc(EA, 1)
                    b.raw("scalar", scan_tanh)
                    b.cnt[EA.name] = b.cnt.get(EA.name, 0) + 2
                b.wait("gpsimd", EV)
                b.dma("gpsimd", SW, st_all.rearrange("c p s -> p c s")[:, :, ch * CR:(ch + 1) * CR],
                      ring_s[:, :, par * CR:par * CR + CR])
                if ch + 2 < NCH:
                    b.wait("sync", EV, ev0 + 4 * CR * (ch + 1))
                    for hbm, ring in ((g_all, ring_g), (c_all, ring_c), (u_all, ring_u)):
                        b.dma("sync", LD, ring[:, :, par * CR:par * CR + CR],
                              hbm[:, :, (ch + 2) * CR:(ch + 3) * CR])
            b.wait("gpsimd", SW)
            b.barrier(PH)

            for dk in range(8):
                b.dma("gpsimd", SW, AW(dk * 1024, 1024), s1_woutT[dk * 128:(dk + 1) * 128, :])
            b.wait("gpsimd", SW)
            b.barrier(PH)
            ntile = (S // 128) if s1has("s1c") else 0
            mm_g, ev_g, ld_v, sw_g = {}, {}, {}, {}
            for tt in range(ntile):
                p = tt % 2
                stW = A16(p * 1024, 1024)
                xt = A32(p * 2048, 1024)
                rt = A32(p * 2048 + 1024, 1024)
                if tt >= 2:
                    b.wait("sync", MM, mm_g[tt - 2])
                    b.wait("sync", EV, ev_g[tt - 2])
                if tt >= 1:
                    b.wait("sync", LD, ld_v[tt - 1])
                b.dma("sync", LD, stW.rearrange("p (c t) -> p c t", c=8),
                      st_all[:, :, tt * 128:(tt + 1) * 128].rearrange("c p t -> p c t"))
                ld_v[tt] = b.dma("sync", LD, xt, xin[tt * 128:(tt + 1) * 128, :])
                b.wait("tensor", LD, ld_v[tt])
                for eb in range(2):
                    bank = psB[:, (eb % 2) * 512:(eb % 2) * 512 + 512]
                    if tt > 0:
                        b.wait("tensor", EV, ev_g[tt - 1])
                    for dk in range(8):
                        b.op("tensor", lambda e, o=bank, l=stW[:, dk * 128:(dk + 1) * 128],
                             r=AW(dk * 1024 + eb * 512, 512), s=(dk == 0):
                             e.matmul(o, l, r, start=s, stop=False))
                    b.op("tensor", lambda e, o=bank, r=rowsb[0:1, eb * 512:(eb + 1) * 512]:
                         e.matmul(o, onesb[0:1, :], r, start=False, stop=True), MM)
                mm_g[tt] = b.cnt[MM.name]
                b.wait("vector", MM, mm_g[tt])
                if tt >= 2:
                    b.wait("vector", SW, sw_g[tt - 2])
                for eb in range(2):
                    bank = psB[:, (eb % 2) * 512:(eb % 2) * 512 + 512]
                    b.op("vector", lambda e, o=rt[:, eb * 512:(eb + 1) * 512], i=bank,
                         x=xt[:, eb * 512:(eb + 1) * 512]: e.tensor_add(o, i, x), EV)
                ev_g[tt] = b.cnt[EV.name]
                b.wait("gpsimd", EV, ev_g[tt])
                if tt >= 1:
                    b.wait("gpsimd", SW, sw_g[tt - 1])
                b.dma("gpsimd", SW, R_hbm[tt * 128:(tt + 1) * 128, :], rt)
                sw_g[tt] = b.cnt[SW.name]
            b.wait("gpsimd", SW)
            b.barrier(PH)

        if "s1" not in phases:
            b.dma("gpsimd", SW, R_hbm, xin)
            b.wait("gpsimd", SW)
            b.barrier(PH)

        c127 = cols[:, 120:121]
        cone = cols[:, 121:122]

        def ffn_weight_prep():
            for wi, (wT, nrow, ncol, wq) in enumerate(
                    ((f_w1T, 1024, 4096, w1q_hbm), (f_w2T, 4096, 1024, w2q_hbm))):
                acc = cols[:, 56:57]
                tmp = cols[:, 57:58]
                b.op("vector", lambda e, a=acc: e.memset(a, 0.0), EV)
                for rc in range(nrow // 128):
                    buf = A32(0, ncol)
                    b.wait("sync", EV)
                    b.wait("sync", LDS)
                    b.dma("sync", LDS, buf, wT[rc * 128:(rc + 1) * 128, :])
                    b.wait("vector", LDS)
                    b.op("vector", lambda e, t=tmp, s=buf: e.tensor_reduce(
                        t, s, AX, ALU.add, apply_absolute_value=True))
                    b.op("vector", lambda e: e.drain())
                    b.op("vector", lambda e, a=acc, t=tmp: e.tensor_add(a, a, t))
                    b.op("vector", lambda e: e.drain())
                    b.op("vector", lambda e: e.engine_nop(), EV)
                b.wait("gpsimd", EV)
                b.dma("gpsimd", SW, sc_hbm[0:128], acc, allow_slow_non_contiguous=True)
                b.wait("gpsimd", SW)
                rowbuf = a32[0:1, 16384:16512]
                b.dma("gpsimd", SW, rowbuf, sc_hbm[0:128])
                b.wait("vector", SW)
                tot = cols[0:1, 58:59]
                invd = cols[0:1, 59:60]
                sw_ = cols[0:1, 60:61]
                b.op("vector", lambda e, o=tot, s=rowbuf: e.tensor_reduce(o, s, AX, ALU.add))
                b.op("vector", lambda e: e.drain())
                b.op("vector", lambda e, o=tot: e.tensor_scalar(
                    o, o, 2.0 ** -22, EPS5, ALU.mult, ALU.max))
                b.op("vector", lambda e: e.drain())
                b.op("vector", lambda e, o=invd, s=tot: e.tensor_scalar(
                    o, s, 0.007874015718698502, None, ALU.mult))
                b.op("vector", lambda e, o=sw_, s=tot: e.reciprocal(o, s))
                b.op("vector", lambda e: e.drain())
                b.op("vector", lambda e: e.engine_nop(), EV)
                b.wait("gpsimd", EV)
                b.dma("gpsimd", SW, sc_hbm[128 + 2 * wi:129 + 2 * wi], sw_)
                b.dma("gpsimd", SW, sc_hbm[129 + 2 * wi:130 + 2 * wi], invd)
                swb = cols[:, 61 + 2 * wi:62 + 2 * wi]
                b.wait("gpsimd", SW)
                b.dma("gpsimd", SW, swb,
                      sc_hbm[128 + 2 * wi:129 + 2 * wi].partition_broadcast(128))
                b.wait("scalar", SW)
                for rc in range(nrow // 128):
                    buf = A32(0, ncol)
                    bufq = A16(0, ncol)
                    b.wait("sync", SW)
                    b.wait("sync", EV)
                    b.wait("sync", LDS)
                    b.dma("sync", LDS, buf, wT[rc * 128:(rc + 1) * 128, :])
                    b.wait("scalar", LDS)
                    b.op("scalar", lambda e, o=buf, s=swb: e.activation(
                        out=o, in_=o, func=AF.Identity, scale=s), EA)
                    b.wait("vector", EA)
                    b.wait("vector", SW)
                    b.op("vector", lambda e, o=buf: e.tensor_scalar(
                        o, o, RNDC, -RNDC, ALU.add, ALU.add))
                    b.op("vector", lambda e: e.drain())
                    b.op("vector", lambda e, o=bufq, s=buf: e.tensor_scalar(
                        o, s, -1.0, 1.0, ALU.max, ALU.min), EV)
                    b.wait("gpsimd", EV)
                    b.dma("gpsimd", SW, wq[rc * 128:(rc + 1) * 128, :], bufq)
                b.wait("gpsimd", SW)
            b.barrier(PH)

        def ffn_consts():
            for wi in range(2):
                b.dma("gpsimd", SW, cols[:, 61 + 2 * wi:62 + 2 * wi],
                      sc_hbm[128 + 2 * wi:129 + 2 * wi].partition_broadcast(128))
                b.dma("gpsimd", SW, cols[:, 62 + 2 * wi:63 + 2 * wi],
                      sc_hbm[129 + 2 * wi:130 + 2 * wi].partition_broadcast(128))
            for off, src, width in ((8192, f_ln1w, 1024), (9216, f_ln1b, 1024),
                                    (10240, f_ln2w, 4096), (14336, f_ln2b, 4096),
                                    (18432, f_b1, 4096), (22528, f_b2, 1024)):
                b.dma("gpsimd", SW, AW(off, width),
                      src[0:1, :].squeeze(0).partition_broadcast(128))
            b.wait("gpsimd", SW)
            b.barrier(PH)

        def emit_ffn():
            M_ln1 = (AW(8192, 1024), AW(9216, 1024))
            M_ln2 = (AW(10240, 4096), AW(14336, 4096))
            M_b1 = AW(18432, 4096)
            M_b2 = AW(22528, 1024)
            NC2 = S // 256
            SLB = (0, 4096, 23552, 27648)
            SSEM = (LDS, TPa, TPb, EG)
            mm_guard = [b.cnt.get(MM.name, 0)] * 4
            lds_val = {}

            def slab_src(k):
                if k < 8:
                    return w1q_hbm[:, k * 512:(k + 1) * 512].rearrange("(c p) n -> p c n", p=128)
                eb, grp = (k - 8) // 4, (k - 8) % 4
                return (w2q_hbm[grp * 1024:(grp + 1) * 1024, eb * 512:(eb + 1) * 512]
                        .rearrange("(c p) n -> p c n", p=128))

            def slab_load(k):
                r = k % 4
                b.wait("sync", MM, mm_guard[r])
                lds_val[k] = b.dma("sync", SSEM[r],
                                   AW(SLB[r], 4096).rearrange("p (c n) -> p c n", c=8),
                                   slab_src(k))

            for ch in range(NC2):
                t0 = ch * 256
                par = ch % 2
                Rt = [A32(par * 15360 + j * 1024, 1024) for j in range(2)]
                for k in range(3):
                    slab_load(k)
                if ch == 0:
                    for j in range(2):
                        b.dma("sync", LD, Rt[j], R_hbm[t0 + j * 128:t0 + (j + 1) * 128, :])
                b.wait("vector", LD)
                b.wait("scalar", LD)
                for j in range(2):
                    xn1 = A32(12288, 1024)
                    xq32 = A32(13312, 1024)
                    emit_ln(Rt[j], xn1, 1024)
                    emit_ln(xn1, xq32, 1024, affine=M_ln1)
                    am = cols[:, 68:69]
                    scol = cols[:, 69:70]
                    c1j = cols[:, 66 + j:67 + j]
                    b.wait("vector", EV)
                    b.wait("vector", EA)
                    b.op("vector", lambda e, o=am, s=xq32: e.tensor_reduce(
                        o, s, AX, ALU.max, apply_absolute_value=True))
                    b.op("vector", lambda e: e.drain())
                    b.op("vector", lambda e, o=am: e.tensor_scalar(
                        o, o, EPS5, None, ALU.max))
                    b.op("vector", lambda e: e.drain())
                    b.op("vector", lambda e, o=c1j, a=am, i=cols[:, 62:63]:
                         e.tensor_mul(o, a, i))
                    b.op("vector", lambda e, o=scol, a=am: e.reciprocal(o, a))
                    b.op("vector", lambda e: e.drain())
                    b.op("vector", lambda e, o=scol: e.tensor_scalar(
                        o, o, 127.0, None, ALU.mult), EV)
                    b.wait("scalar", EV)
                    b.op("scalar", lambda e, o=xq32, s=scol: e.activation(
                        out=o, in_=o, func=AF.Identity, scale=s), EA)
                    b.wait("vector", EA)
                    xqstage = A16(10240, 1024)
                    b.wait("vector", TP)
                    b.op("vector", lambda e, o=xq32: e.tensor_scalar(
                        o, o, RNDC, -RNDC, ALU.add, ALU.add))
                    b.op("vector", lambda e: e.drain())
                    b.op("vector", lambda e, o=xqstage, s=xq32: e.tensor_scalar(
                        o, s, -128.0, 127.0, ALU.max, ALU.min), EV)
                    b.wait("scalar", EV)
                    b.wait("scalar", MM)
                    b.dma_tr("scalar", TP,
                             A16(j * 1024, 1024).rearrange("p (c n) -> p c n", c=8),
                             xqstage.rearrange("p (c n) -> p c n", c=8))
                if ch + 1 < NC2:
                    npar = (ch + 1) % 2
                    b.wait("sync", SW)
                    b.wait("sync", LD)
                    for j in range(2):
                        b.dma("sync", LD, A32(npar * 15360 + j * 1024, 1024),
                              R_hbm[(ch + 1) * 256 + j * 128:(ch + 1) * 256 + (j + 1) * 128, :])
                h1 = [A32(2048 + j * 4096, 4096) for j in range(2)]
                b.wait("tensor", TP)
                for hb in range(8):
                    if hb + 3 < 16:
                        slab_load(hb + 3)
                    r = hb % 4
                    slab = AW(SLB[r], 4096)
                    b.wait("tensor", SSEM[r], lds_val[hb])
                    for j in range(2):
                        bank = psB[:, ((hb % 2) * 2 + j) * 512:((hb % 2) * 2 + j) * 512 + 512]
                        b.wait("tensor", EV)
                        for dk in range(8):
                            b.op("tensor", lambda e, o=bank, l=A16(j * 1024 + dk * 128, 128),
                                 r=slab[:, dk * 512:(dk + 1) * 512], s=(dk == 0), p=(dk == 7):
                                 e.matmul(o, l, r, start=s, stop=p), MM if dk == 7 else None)
                        b.wait("vector", MM)
                        b.op("vector", lambda e, o=h1[j][:, hb * 512:(hb + 1) * 512], i=bank,
                             c=cols[:, 66 + j:67 + j], bm=M_b1[:, hb * 512:(hb + 1) * 512]:
                             e.scalar_tensor_tensor(o, i, c, bm, ALU.mult, ALU.add), EV)
                        b.wait("scalar", EV)
                        b.op("scalar", lambda e, o=h1[j][:, hb * 512:(hb + 1) * 512]:
                             e.activation(out=o, in_=o, func=AF.Gelu), EA)
                    mm_guard[r] = b.cnt[MM.name]
                for j in range(2):
                    b.wait("vector", EA)
                    emit_ln(h1[j], h1[j], 4096, affine=M_ln2)
                    am = cols[:, 68:69]
                    scol = cols[:, 69:70]
                    c2j = cols[:, 70 + j:71 + j]
                    b.wait("vector", EV)
                    b.wait("vector", EA)
                    b.op("vector", lambda e, o=am, s=h1[j]: e.tensor_reduce(
                        o, s, AX, ALU.max, apply_absolute_value=True))
                    b.op("vector", lambda e: e.drain())
                    b.op("vector", lambda e, o=am: e.tensor_scalar(
                        o, o, EPS5, None, ALU.max))
                    b.op("vector", lambda e: e.drain())
                    b.op("vector", lambda e, o=c2j, a=am, i=cols[:, 64:65]:
                         e.tensor_mul(o, a, i))
                    b.op("vector", lambda e, o=scol, a=am: e.reciprocal(o, a))
                    b.op("vector", lambda e: e.drain())
                    b.op("vector", lambda e, o=scol: e.tensor_scalar(
                        o, o, 127.0, None, ALU.mult), EV)
                    b.wait("scalar", EV)
                    b.op("scalar", lambda e, o=h1[j], s=scol: e.activation(
                        out=o, in_=o, func=AF.Identity, scale=s), EA)
                    b.wait("vector", EA)
                    hstage = A16(10240, 4096)
                    b.wait("vector", TP)
                    b.op("vector", lambda e, o=h1[j]: e.tensor_scalar(
                        o, o, RNDC, -RNDC, ALU.add, ALU.add))
                    b.op("vector", lambda e: e.drain())
                    b.op("vector", lambda e, o=hstage, s=h1[j]: e.tensor_scalar(
                        o, s, -128.0, 127.0, ALU.max, ALU.min), EV)
                    b.wait("scalar", EV)
                    b.dma_tr("scalar", TP,
                             A16(2048 + j * 4096, 4096).rearrange("p (c n) -> p c n", c=32),
                             hstage.rearrange("p (c n) -> p c n", c=32))
                b.wait("tensor", TP)
                for eb in range(2):
                    for grp in range(4):
                        k = 8 + eb * 4 + grp
                        if k + 3 < 16:
                            slab_load(k + 3)
                        r = k % 4
                        slab = AW(SLB[r], 4096)
                        b.wait("tensor", SSEM[r], lds_val[k])
                        for j in range(2):
                            bank = psB[:, (eb * 2 + j) * 512:(eb * 2 + j) * 512 + 512]
                            if grp == 0:
                                b.wait("tensor", EV)
                            for kc in range(8):
                                b.op("tensor", lambda e, o=bank,
                                     l=A16(2048 + j * 4096 + (grp * 8 + kc) * 128, 128),
                                     r=slab[:, kc * 512:(kc + 1) * 512],
                                     s=(grp == 0 and kc == 0), p=(grp == 3 and kc == 7):
                                     e.matmul(o, l, r, start=s, stop=p),
                                     MM if kc == 7 else None)
                        mm_guard[r] = b.cnt[MM.name]
                    for j in range(2):
                        bank = psB[:, (eb * 2 + j) * 512:(eb * 2 + j) * 512 + 512]
                        rtmp = A32(14336, 512)
                        b.wait("vector", MM)
                        b.op("vector", lambda e, o=rtmp, i=bank, c=cols[:, 70 + j:71 + j],
                             bm=M_b2[:, eb * 512:(eb + 1) * 512]:
                             e.scalar_tensor_tensor(o, i, c, bm, ALU.mult, ALU.add))
                        b.op("vector", lambda e: e.drain())
                        b.op("vector", lambda e, o=Rt[j][:, eb * 512:(eb + 1) * 512], t=rtmp:
                             e.tensor_add(o, o, t))
                        b.op("vector", lambda e: e.drain())
                        b.op("vector", lambda e: e.engine_nop(), EV)
                b.wait("gpsimd", EV)
                for j in range(2):
                    b.dma("gpsimd", SW, R_hbm[t0 + j * 128:t0 + (j + 1) * 128, :], Rt[j])
            b.wait("gpsimd", SW)
            b.barrier(PH)

        if "s2" in phases:
            ffn_weight_prep()
            ffn_consts()
            emit_ffn()

        def emit_s3():
            for dk in range(8):
                b.dma("gpsimd", SW, AW(dk * 4096, 4096), la_wcat[dk * 128:(dk + 1) * 128, :])
            for coff, src in ((72, la_qb), (80, la_kb), (88, la_gb)):
                b.dma("sync", LD, cols[:, coff:coff + 8],
                      src.rearrange("c p one -> p (c one)"), allow_slow_non_contiguous=True)
            b.dma("gpsimd", SW, a16[:, 12288:12416], triu128)
            for off, src in ((13952, la_lng), (14976, la_lnb)):
                b.dma("gpsimd", SW, A16(off, 1024),
                      src[0:1, :].squeeze(0).partition_broadcast(128))
            b.dma("gpsimd", SW, rowsb[0:1, 1024:2048], la_vb_row)
            S_f32 = A32(10240, 8192)
            carry = a32[0:1, 18432:19456]
            b.op("vector", lambda e: e.memset(S_f32, 0.0))
            b.op("vector", lambda e: e.memset(carry, 0.0))
            b.op("vector", lambda e: e.drain())
            b.op("vector", lambda e: e.memset(rowsb[0:1, 2048:3072], 0.0), EV)
            b.wait("gpsimd", SW)
            b.wait("sync", LD)
            b.barrier(PH)
            M3 = (A16(13952, 1024), A16(14976, 1024))
            triu = a16[:, 12288:12416]
            carry16 = rowsb[0:1, 2048:3072]
            NC3 = S // 256
            for ch in range(NC3):
                t0 = ch * 256
                Rt = [A32(j * 1024, 1024) for j in range(2)]
                for j in range(2):
                    b.dma("sync", LD, Rt[j], R_hbm[t0 + j * 128:t0 + (j + 1) * 128, :])
                b.wait("vector", LD)
                b.wait("scalar", LD)
                for j in range(2):
                    xn1 = A32(4096, 1024)
                    xn2 = A32(6144, 1024)
                    emit_ln(Rt[j], xn1, 1024)
                    emit_ln(xn1, xn2, 1024, affine=M3)
                    b.wait("vector", EV)
                    b.wait("vector", EA)
                    st16 = A16(10240, 1024)
                    b.wait("vector", TP)
                    b.op("vector", lambda e, o=st16, s=xn2: e.tensor_copy(o, s), EV)
                    b.wait("scalar", EV)
                    b.dma_tr("scalar", TP,
                             A16(0, 2048).rearrange("p (c n) -> p c n", c=8)[:, :, j * 128:(j + 1) * 128],
                             st16.rearrange("p (c n) -> p c n", c=8))
                b.wait("tensor", TP)
                xn2F = A16(0, 2048).rearrange("p (c t) -> p c t", c=8)
                def proj_F(ocbase, dstF, biascol, actf):
                    for oc in range(8):
                        bank = psB[:, (oc % 4) * 512:(oc % 4) * 512 + 256]
                        b.wait("tensor", EA)
                        for dk in range(8):
                            b.op("tensor", lambda e, o=bank,
                                 l=AW(dk * 4096 + (ocbase + oc) * 128, 128),
                                 r=a16[:, dk * 256:(dk + 1) * 256],
                                 s=(dk == 0), p=(dk == 7):
                                 e.matmul(o, l, r, start=s, stop=p), MM if dk == 7 else None)
                        b.wait("scalar", MM)
                        b.op("scalar", lambda e, o=dstF[:, oc * 256:(oc + 1) * 256], i=bank,
                             f=actf, bi=cols[:, biascol + oc:biascol + oc + 1]:
                             e.activation(out=o, in_=i, func=f, bias=bi), EA)
                gF = A32(2048, 2048)
                proj_F(24, gF, 88, AF.Sigmoid)
                qraw = A32(4096, 2048)
                proj_F(0, qraw, 72, AF.Identity)
                b.wait("vector", EA)
                wk2 = A32(6144, 2048)
                b.op("vector", lambda e, o=wk2, s=qraw: e.tensor_scalar(
                    o, s, 0.0, None, ALU.min))
                b.op("vector", lambda e: e.drain())
                b.op("vector", lambda e: e.engine_nop(), EV)
                b.wait("scalar", EV)
                b.op("scalar", lambda e, o=wk2: e.activation(out=o, in_=o, func=AF.Exp), EA)
                b.wait("vector", EA)
                b.op("vector", lambda e, o=qraw: e.tensor_scalar(o, o, 0.0, None, ALU.max))
                b.op("vector", lambda e: e.drain())
                b.op("vector", lambda e, o=A16(2048, 2048), s=qraw, t=wk2:
                     e.tensor_add(o, s, t), EV)
                kraw = A32(4096, 2048)
                b.wait("tensor", EV)
                proj_F(8, kraw, 80, AF.Identity)
                b.wait("vector", EA)
                b.op("vector", lambda e, o=kraw, g2=gF: e.tensor_mul(o, o, g2))
                b.op("vector", lambda e: e.drain())
                b.op("vector", lambda e, o=wk2, s=kraw: e.tensor_scalar(
                    o, s, 0.0, None, ALU.min))
                b.op("vector", lambda e: e.drain())
                b.op("vector", lambda e: e.engine_nop(), EV)
                b.wait("scalar", EV)
                b.op("scalar", lambda e, o=wk2: e.activation(out=o, in_=o, func=AF.Exp), EA)
                b.wait("vector", EA)
                b.op("vector", lambda e, o=kraw: e.tensor_scalar(o, o, 0.0, None, ALU.max))
                b.op("vector", lambda e: e.drain())
                b.op("vector", lambda e, o=A16(4096, 2048), s=kraw, t=wk2:
                     e.tensor_add(o, s, t), EV)
                b.wait("scalar", EV)
                for j in range(2):
                    for dk in range(8):
                        b.dma_tr("scalar", TP,
                                 a16[:, 6144 + j * 1024 + dk * 128:6144 + j * 1024 + (dk + 1) * 128],
                                 a16[:, 4096 + dk * 256 + j * 128:4096 + dk * 256 + (j + 1) * 128])
                        b.dma_tr("scalar", TP,
                                 a16[:, 10240 + j * 1024 + dk * 128:10240 + j * 1024 + (dk + 1) * 128],
                                 a16[:, 2048 + dk * 256 + j * 128:2048 + dk * 256 + (j + 1) * 128])
                b.wait("tensor", TP)
                b.wait("vector", TP)
                b.barrier(PH)
                for j in range(2):
                    for dvh in range(2):
                        bank = psB[:, (j * 2 + dvh) * 512:(j * 2 + dvh) * 512 + 512]
                        for dk in range(8):
                            b.op("tensor", lambda e, o=bank,
                                 l=a16[:, dk * 256 + j * 128:dk * 256 + (j + 1) * 128],
                                 r=AW(dk * 4096 + 2048 + dvh * 512, 512),
                                 s=(dk == 0): e.matmul(o, l, r, start=s, stop=False))
                        b.op("tensor", lambda e, o=bank,
                             r=rowsb[0:1, 1024 + dvh * 512:1024 + (dvh + 1) * 512]:
                             e.matmul(o, onesb[0:1, :], r, start=False, stop=True),
                             MM)
                b.wait("vector", MM)
                for j in range(2):
                    for dvh in range(2):
                        bank = psB[:, (j * 2 + dvh) * 512:(j * 2 + dvh) * 512 + 512]
                        b.op("vector", lambda e, o=a16[:, 8192 + j * 1024 + dvh * 512:
                                                       8192 + j * 1024 + (dvh + 1) * 512],
                             i=bank: e.tensor_copy(o, i), EV)
                b.barrier(PH)
                pts = {}
                nscore = 0
                for j in range(2):
                    for i in range(j + 1):
                        bank = psB[:, (nscore % 4) * 512:(nscore % 4) * 512 + 128]
                        for dk in range(8):
                            b.op("tensor", lambda e, o=bank,
                                 l=a16[:, 4096 + dk * 256 + i * 128:4096 + dk * 256 + (i + 1) * 128],
                                 r=a16[:, 2048 + dk * 256 + j * 128:2048 + dk * 256 + (j + 1) * 128],
                                 s=(dk == 0), p=(dk == 7):
                                 e.matmul(o, l, r, start=s, stop=p), MM if dk == 7 else None)
                        pt = a16[:, 16000 + nscore * 128:16000 + (nscore + 1) * 128]
                        pts[(j, i)] = pt
                        b.wait("vector", MM)
                        if i == j:
                            b.op("vector", lambda e, o=pt, i2=bank, t=triu:
                                 e.tensor_mul(o, i2, t), EV)
                        else:
                            b.op("vector", lambda e, o=pt, i2=bank: e.tensor_copy(o, i2), EV)
                        nscore += 1
                for j in range(2):
                    for half in range(2):
                        bank = psB[:, (j * 2 + half) * 512:(j * 2 + half) * 512 + 512]
                        b.wait("tensor", EV)
                        b.op("tensor", lambda e, o=bank, t=triu,
                             r=a16[:, 6144 + j * 1024 + half * 512:6144 + j * 1024 + (half + 1) * 512]:
                             e.matmul(o, t, r, start=True, stop=False))
                        b.op("tensor", lambda e, o=bank,
                             r=carry16[0:1, half * 512:(half + 1) * 512]:
                             e.matmul(o, onesb[0:1, :], r, start=False, stop=True), MM)
                        b.wait("vector", MM)
                        b.op("vector", lambda e, o=A32(8192 + j * 1024 + half * 512, 512),
                             i=bank: e.tensor_copy(o, i), EV)
                    for half in range(2):
                        cps = psS[0:1, half * 512:(half + 1) * 512]
                        b.op("tensor", lambda e, o=cps,
                             r=a16[:, 6144 + j * 1024 + half * 512:6144 + j * 1024 + (half + 1) * 512]:
                             e.matmul(o, ones128[:, 0:1], r, start=True, stop=True), MM)
                    b.wait("vector", MM)
                    for half in range(2):
                        cps = psS[0:1, half * 512:(half + 1) * 512]
                        b.op("vector", lambda e, o=carry[0:1, half * 512:(half + 1) * 512],
                             i=cps: e.tensor_add(o, o, i))
                    b.op("vector", lambda e: e.drain())
                    b.op("vector", lambda e, o=carry16, s=carry: e.tensor_copy(o, s), EV)
                b.barrier(PH)
                for j in range(2):
                    wk1 = A32(4096, 1024)
                    b.op("vector", lambda e, o=wk1, q=a16[:, 10240 + j * 1024:10240 + (j + 1) * 1024],
                         k2=A32(8192 + j * 1024, 1024): e.tensor_mul(o, q, k2))
                    b.op("vector", lambda e: e.drain())
                    b.op("vector", lambda e, o=cols[:, 96 + j:97 + j], s=wk1:
                         e.tensor_reduce(o, s, AX, ALU.add))
                    b.op("vector", lambda e: e.drain())
                    b.op("vector", lambda e, o=cols[:, 96 + j:97 + j]: e.tensor_scalar(
                        o, o, 1e-6, None, ALU.add))
                    b.op("vector", lambda e: e.drain())
                    b.op("vector", lambda e, o=cols[:, 98 + j:99 + j],
                         s=cols[:, 96 + j:97 + j]: e.reciprocal(o, s), EV)
                b.barrier(PH)
                mm_at_use = {}
                for dvh in range(2):
                    for dk in range(8):
                        u = dvh * 8 + dk
                        piece = A16(12416 + (u % 3) * 512, 512)
                        if u >= 3:
                            b.wait("vector", MM, mm_at_use[u - 3])
                        b.op("vector", lambda e, o=piece,
                             s=A32(10240 + dk * 1024 + dvh * 512, 512):
                             e.tensor_copy(o, s), EV)
                        b.wait("tensor", EV)
                        for j in range(2):
                            bank = psS[:, (j * 2 + dvh) * 512:(j * 2 + dvh) * 512 + 512]
                            b.op("tensor", lambda e, o=bank,
                                 l=a16[:, 2048 + dk * 256 + j * 128:2048 + dk * 256 + (j + 1) * 128],
                                 r=piece, s=(dk == 0):
                                 e.matmul(o, l, r, start=s, stop=False),
                                 MM if j == 1 else None)
                        mm_at_use[u] = b.cnt[MM.name]
                    for j in range(2):
                        bank = psS[:, (j * 2 + dvh) * 512:(j * 2 + dvh) * 512 + 512]
                        for i in range(j + 1):
                            b.op("tensor", lambda e, o=bank, l=pts[(j, i)],
                                 r=a16[:, 8192 + i * 1024 + dvh * 512:8192 + i * 1024 + (dvh + 1) * 512],
                                 p=(i == j): e.matmul(o, l, r, start=False, stop=p),
                                 MM if i == j else None)
                b.wait("vector", MM)
                for j in range(2):
                    for dvh in range(2):
                        bank = psS[:, (j * 2 + dvh) * 512:(j * 2 + dvh) * 512 + 512]
                        b.op("vector", lambda e, o=Rt[j][:, dvh * 512:(dvh + 1) * 512], i=bank,
                             dn=cols[:, 98 + j:99 + j]:
                             e.scalar_tensor_tensor(o, i, dn, o, ALU.mult, ALU.add), EV)
                b.barrier(PH)
                for dvh in range(2):
                    if dvh == 1:
                        b.wait("tensor", EV)
                    for dk in range(8):
                        bank = (psS[:, dk * 512:dk * 512 + 512] if dk < 4
                                else psB[:, (dk - 4) * 512:(dk - 4) * 512 + 512])
                        for j in range(2):
                            b.op("tensor", lambda e, o=bank,
                                 l=a16[:, 6144 + j * 1024 + dk * 128:6144 + j * 1024 + (dk + 1) * 128],
                                 r=a16[:, 8192 + j * 1024 + dvh * 512:8192 + j * 1024 + (dvh + 1) * 512],
                                 s=(j == 0), p=(j == 1):
                                 e.matmul(o, l, r, start=s, stop=p), MM if j == 1 else None)
                    b.wait("vector", MM)
                    for dk in range(8):
                        bank = (psS[:, dk * 512:dk * 512 + 512] if dk < 4
                                else psB[:, (dk - 4) * 512:(dk - 4) * 512 + 512])
                        b.op("vector", lambda e, o=A32(10240 + dk * 1024 + dvh * 512, 512),
                             i=bank: e.tensor_add(o, o, i), EV)
                b.wait("gpsimd", EV)
                for j in range(2):
                    b.dma("gpsimd", SW, R_hbm[t0 + j * 128:t0 + (j + 1) * 128, :], Rt[j])
                b.wait("gpsimd", SW)
                b.barrier(PH)

        if "s3" in phases:
            emit_s3()

        if "s4" in phases:
            if "s2" not in phases:
                ffn_weight_prep()
            ffn_consts()
            emit_ffn()

        def emit_s5():
            for dk in range(8):
                b.dma("gpsimd", SW, AW(dk * 3072, 3072), sw_winT[dk * 128:(dk + 1) * 128, :])
                b.dma("gpsimd", SW, AW(24576 + dk * 1024, 1024), sw_woutT[dk * 128:(dk + 1) * 128, :])
            b.op("vector", lambda e: e.memset(a16[:, 8192:24576], 0.0), EV)
            b.wait("gpsimd", SW)
            b.barrier(PH)
            slopes = [float(np.exp2(np.float32(-(h + 1) * 8.0 / 16.0))) for h in range(16)]
            NC5 = S // 512
            for ch in range(NC5):
                t0 = ch * 512
                par, prv = ch % 2, 1 - ch % 2
                Rt = [A32(j * 1024, 1024) for j in range(4)]
                for j in range(4):
                    b.dma("sync", LD, Rt[j], R_hbm[t0 + j * 128:t0 + (j + 1) * 128, :])
                if ch <= 1:
                    for j in range(4):
                        b.dma("sync", LD, A32(4096 + j * 1024, 1024),
                              sw_tab[min(ch, 1)][j * 128:(j + 1) * 128, :])
                b.wait("vector", LD)
                b.wait("scalar", LD)
                for j in range(4):
                    xn1 = A32(8192, 1024)
                    emit_ln(Rt[j], xn1, 1024)
                    st16 = A16(4096, 1024)
                    b.wait("vector", EA)
                    b.wait("vector", TP)
                    b.op("vector", lambda e, o=st16, s=xn1: e.tensor_copy(o, s), EV)
                    b.wait("scalar", EV)
                    b.dma_tr("scalar", TP,
                             A16(0, 4096).rearrange("p (c n) -> p c n", c=8)[:, :, j * 128:(j + 1) * 128],
                             st16.rearrange("p (c n) -> p c n", c=8))
                b.wait("tensor", TP)
                for oc in range(16):
                    bank = psB[:, (oc % 4) * 512:(oc % 4) * 512 + 512]
                    b.wait("tensor", EA)
                    for dk in range(8):
                        b.op("tensor", lambda e, o=bank,
                             l=AW(dk * 3072 + oc * 128, 128),
                             r=a16[:, dk * 512:(dk + 1) * 512],
                             s=(dk == 0), p=(dk == 7):
                             e.matmul(o, l, r, start=s, stop=p), MM if dk == 7 else None)
                    b.wait("scalar", MM)
                    if oc < 8:
                        b.op("scalar", lambda e, o=a16[:, 4096 + oc * 512:4096 + (oc + 1) * 512],
                             i=bank: e.activation(out=o, in_=i, func=AF.Identity, scale=0.125), EA)
                    else:
                        b.op("scalar", lambda e, o=a16[:, 8192 + par * 4096 + (oc - 8) * 512:
                                                       8192 + par * 4096 + (oc - 7) * 512],
                             i=bank: e.activation(out=o, in_=i, func=AF.Identity), EA)
                for j in range(4):
                    for vh in range(2):
                        bank = psS[:, ((j * 2 + vh) % 4) * 512:((j * 2 + vh) % 4) * 512 + 512]
                        b.wait("tensor", EV)
                        for dk in range(8):
                            b.op("tensor", lambda e, o=bank,
                                 l=a16[:, dk * 512 + j * 128:dk * 512 + (j + 1) * 128],
                                 r=AW(dk * 3072 + 2048 + vh * 512, 512),
                                 s=(dk == 0), p=(dk == 7):
                                 e.matmul(o, l, r, start=s, stop=p), MM if dk == 7 else None)
                        b.wait("vector", MM)
                        b.op("vector", lambda e, o=a16[:, 16384 + par * 4096 + j * 1024 + vh * 512:
                                                       16384 + par * 4096 + j * 1024 + (vh + 1) * 512],
                             i=bank: e.tensor_copy(o, i), EV)
                b.barrier(PH)
                for j in range(4):
                    outbank = [psS[:, 0:512], psS[:, 512:1024]]
                    mm_sc, mm_pv, ev_stt, tp_h, ea_h = {}, {}, {}, {}, {}

                    def s5_scores(h):
                        p = h % 2
                        if h >= 2:
                            b.wait("tensor", EV, ev_stt[h - 2])
                        for half, kpar in ((0, prv), (1, par)):
                            bank = psB[:, p * 1024 + half * 512:p * 1024 + (half + 1) * 512]
                            b.op("tensor", lambda e, o=bank,
                                 l=a16[64 * (h % 2):64 * (h % 2) + 64,
                                       4096 + (h // 2) * 512 + j * 128:4096 + (h // 2) * 512 + (j + 1) * 128],
                                 r=a16[64 * (h % 2):64 * (h % 2) + 64,
                                       8192 + kpar * 4096 + (h // 2) * 512:8192 + kpar * 4096 + (h // 2 + 1) * 512]:
                                 e.matmul(o, l, r, start=True, stop=True), MM)
                        mm_sc[h] = b.cnt[MM.name]
                        p16 = A16(p * 1024, 1024)
                        s32 = A32(8192 + p * 1024, 1024)
                        nm = cols[:, 104:105]
                        rs = cols[:, 105:106]
                        b.wait("vector", MM, mm_sc[h])
                        for half in range(2):
                            bank = psB[:, p * 1024 + half * 512:p * 1024 + (half + 1) * 512]
                            b.op("vector", lambda e, o=s32[:, half * 512:(half + 1) * 512],
                                 t=A32(4096 + j * 1024 + half * 512, 512), sl=slopes[h], i=bank:
                                 e.scalar_tensor_tensor(o, t, sl, i, ALU.mult, ALU.add), EV)
                        ev_stt[h] = b.cnt[EV.name]
                        b.op("vector", lambda e: e.drain())
                        b.op("vector", lambda e, o=nm, s=s32: e.tensor_reduce(
                            o, s, AX, ALU.max, negate=True), EV)
                        b.wait("scalar", EV)
                        if h >= 2:
                            b.wait("scalar", TPa if p == 0 else TPb, tp_h[h - 2])
                        b.op("scalar", lambda e, o=p16, s=s32, m=nm: e.activation(
                            out=o, in_=s, func=AF.Exp, bias=m), EA)
                        ea_h[h] = b.cnt[EA.name]
                        b.wait("vector", EA, ea_h[h])
                        b.op("vector", lambda e, o=rs, s=p16: e.tensor_reduce(o, s, AX, ALU.add))
                        b.op("vector", lambda e: e.drain())
                        b.op("vector", lambda e, o=rs: e.reciprocal(o, o))
                        b.op("vector", lambda e: e.drain())
                        if h >= 2:
                            b.wait("vector", TPa if p == 0 else TPb, tp_h[h - 2])
                        b.op("vector", lambda e, o=p16, r=rs: e.tensor_scalar(
                            o, o, r, None, ALU.mult), EV)
                        b.wait("scalar", EV)
                        if h >= 2:
                            b.wait("scalar", MM, mm_pv[h - 2])
                            b.wait("scalar", TPa if p == 0 else TPb, tp_h[h - 2])
                        b.dma_tr("scalar", TPa if p == 0 else TPb,
                                 A16(2048 + p * 1024, 1024).rearrange("p (c n) -> p c n", c=8),
                                 p16.rearrange("p (c n) -> p c n", c=8))
                        tp_h[h] = b.cnt[(TPa if p == 0 else TPb).name]

                    def s5_pv(h):
                        p = h % 2
                        b.wait("tensor", TPa if p == 0 else TPb, tp_h[h])
                        for kt in range(8):
                            kpar = prv if kt < 4 else par
                            vt = kt % 4
                            b.op("tensor", lambda e, o=outbank[h // 8][:, (h % 8) * 64:(h % 8) * 64 + 64],
                                 l=A16(2048 + p * 1024 + kt * 128, 128),
                                 r=a16[:, 16384 + kpar * 4096 + vt * 1024 + h * 64:
                                       16384 + kpar * 4096 + vt * 1024 + (h + 1) * 64],
                                 s=(kt == 0), p2=(kt == 7):
                                 e.matmul(o, l, r, start=s, stop=p2), MM if kt == 7 else None)
                        mm_pv[h] = b.cnt[MM.name]

                    s5_scores(0)
                    for h in range(1, 16):
                        s5_scores(h)
                        s5_pv(h - 1)
                    s5_pv(15)
                    b.wait("vector", MM)
                    b.wait("vector", TPa)
                    b.wait("vector", TPb)
                    o16 = A16(0, 1024)
                    for hb in range(2):
                        b.op("vector", lambda e, o=o16[:, hb * 512:(hb + 1) * 512],
                             i=outbank[hb]: e.tensor_copy(o, i), EV)
                    b.wait("scalar", EV)
                    b.wait("scalar", MM)
                    b.dma_tr("scalar", TP,
                             A16(1024, 1024).rearrange("p (c n) -> p c n", c=8),
                             o16.rearrange("p (c n) -> p c n", c=8))
                    b.wait("tensor", TP)
                    for eb in range(2):
                        bank = psS[:, 1024 + eb * 512:1024 + eb * 512 + 512]
                        b.wait("tensor", EV)
                        for dk in range(8):
                            b.op("tensor", lambda e, o=bank, l=A16(1024 + dk * 128, 128),
                                 r=AW(24576 + dk * 1024 + eb * 512, 512),
                                 s=(dk == 0), p=(dk == 7):
                                 e.matmul(o, l, r, start=s, stop=p), MM if dk == 7 else None)
                        b.wait("vector", MM)
                        b.op("vector", lambda e, o=Rt[j][:, eb * 512:(eb + 1) * 512],
                             i=bank: e.tensor_add(o, o, i), EV)
                b.wait("gpsimd", EV)
                for j in range(4):
                    b.dma("gpsimd", SW, R_hbm[t0 + j * 128:t0 + (j + 1) * 128, :], Rt[j])
                b.wait("gpsimd", SW)
                b.barrier(PH)

        if "s5" in phases:
            emit_s5()

        def s6_prep():
            for rc in range(8):
                buf = A32(8192, 1024)
                am = cols[:, 106:107]
                b.wait("sync", EV)
                b.wait("sync", LDS)
                b.dma("sync", LDS, buf, q_rhsT[rc * 128:(rc + 1) * 128, :])
                b.wait("vector", LDS)
                b.wait("vector", SW)
                b.op("vector", lambda e, o=am, s=buf: e.tensor_reduce(
                    o, s, AX, ALU.max, apply_absolute_value=True))
                b.op("vector", lambda e: e.drain())
                b.op("vector", lambda e: e.engine_nop(), EV)
                b.wait("gpsimd", EV)
                b.dma("gpsimd", SW, sc_hbm[256 + rc * 128:256 + (rc + 1) * 128], am,
                      allow_slow_non_contiguous=True)
            b.wait("gpsimd", SW)
            rowf = a32[0:1, 8192:9216]
            b.dma("gpsimd", SW, rowf, sc_hbm[256:1280])
            b.wait("vector", SW)
            b.op("vector", lambda e, o=rowf: e.tensor_scalar(
                o, o, 1e-12, 0.007874015718698502, ALU.max, ALU.mult))
            b.op("vector", lambda e: e.drain())
            b.op("vector", lambda e: e.engine_nop(), EV)
            b.wait("gpsimd", EV)
            b.dma("gpsimd", SW, sc_hbm[1024:2048], rowf)
            b.wait("gpsimd", SW)
            rsmat = A32(4096, 1024)
            b.dma("gpsimd", SW, rsmat, sc_hbm[1024:2048].partition_broadcast(128))
            b.wait("sync", SW)
            b.wait("vector", SW)
            rimat = A32(5120, 1024)
            b.op("vector", lambda e, o=rimat, s=rsmat: e.reciprocal(o, s))
            b.op("vector", lambda e: e.drain())
            for rc in range(8):
                buf = A32(8192, 1024)
                bufq = A16(0, 1024)
                b.wait("sync", SW)
                b.wait("sync", EV)
                b.wait("sync", LDS)
                b.dma("sync", LDS, buf, q_rhs[rc * 128:(rc + 1) * 128, :])
                b.wait("vector", LDS)
                b.op("vector", lambda e, o=buf, r=rimat: e.tensor_tensor(o, o, r, ALU.mult))
                b.op("vector", lambda e: e.drain())
                b.op("vector", lambda e, o=buf: e.tensor_scalar(
                    o, o, RNDC, -RNDC, ALU.add, ALU.add))
                b.op("vector", lambda e: e.drain())
                b.op("vector", lambda e, o=bufq, s=buf: e.tensor_scalar(
                    o, s, -127.0, 127.0, ALU.max, ALU.min), EV)
                b.wait("gpsimd", EV)
                b.dma("gpsimd", SW, qrhs_hbm[rc * 128:(rc + 1) * 128, :], bufq)
            b.wait("gpsimd", SW)
            b.barrier(PH)

        def emit_s6():
            for rc in range(8):
                b.dma("gpsimd", SW, AW(rc * 1024, 1024), qrhs_hbm[rc * 128:(rc + 1) * 128, :])
            rsmat = A32(4096, 1024)
            b.dma("gpsimd", SW, rsmat, sc_hbm[1024:2048].partition_broadcast(128))
            b.wait("gpsimd", SW)
            b.barrier(PH)
            NC6 = S // 512
            for ch in range(NC6):
                t0 = ch * 512
                Rt = [A32(j * 1024, 1024) for j in range(4)]
                for j in range(4):
                    b.dma("sync", LD, Rt[j], R_hbm[t0 + j * 128:t0 + (j + 1) * 128, :])
                b.wait("vector", LD)
                b.wait("scalar", LD)
                for j in range(4):
                    xn = A32(5120, 1024)
                    xq = A32(6144, 1024)
                    emit_ln(Rt[j], xn, 1024)
                    am = cols[:, 106:107]
                    lsj = cols[:, 107 + j:108 + j]
                    b.wait("vector", EA)
                    b.op("vector", lambda e, o=am, s=xn: e.tensor_reduce(
                        o, s, AX, ALU.max, apply_absolute_value=True))
                    b.op("vector", lambda e: e.drain())
                    b.op("vector", lambda e, o=lsj, a=am: e.tensor_scalar(
                        o, a, 1e-12, 0.007874015718698502, ALU.max, ALU.mult))
                    b.op("vector", lambda e: e.drain())
                    b.op("vector", lambda e, o=cols[:, 112:113], s=lsj: e.reciprocal(o, s))
                    b.op("vector", lambda e: e.drain())
                    b.op("vector", lambda e, o=xq, s=xn, l=cols[:, 112:113]: e.tensor_scalar(
                        o, s, l, None, ALU.mult))
                    b.op("vector", lambda e: e.drain())
                    b.op("vector", lambda e, o=xq: e.tensor_scalar(
                        o, o, RNDC, -RNDC, ALU.add, ALU.add))
                    b.op("vector", lambda e: e.drain())
                    st16 = A16(1024, 1024)
                    b.wait("vector", TP)
                    b.op("vector", lambda e, o=st16, s=xq: e.tensor_scalar(
                        o, s, -127.0, 127.0, ALU.max, ALU.min), EV)
                    b.wait("scalar", EV)
                    b.dma_tr("scalar", TP,
                             A16(2048 + j * 1024, 1024).rearrange("p (c n) -> p c n", c=8),
                             st16.rearrange("p (c n) -> p c n", c=8))
                b.wait("tensor", TP)
                for j in range(4):
                    for eb in range(2):
                        bank = (psB if j % 2 == 0 else psS)[:, eb * 512:eb * 512 + 512]
                        b.wait("tensor", EV)
                        for dk in range(8):
                            b.op("tensor", lambda e, o=bank,
                                 l=A16(2048 + j * 1024 + dk * 128, 128),
                                 r=AW(dk * 1024 + eb * 512, 512),
                                 s=(dk == 0), p=(dk == 7):
                                 e.matmul(o, l, r, start=s, stop=p), MM if dk == 7 else None)
                        b.wait("vector", MM)
                        tmp = A32(7168, 512)
                        b.op("vector", lambda e, o=tmp, i=bank, l=cols[:, 107 + j:108 + j],
                             r=rsmat[:, eb * 512:(eb + 1) * 512]:
                             e.scalar_tensor_tensor(o, i, l, r, ALU.mult, ALU.mult))
                        b.op("vector", lambda e: e.drain())
                        b.op("vector", lambda e, o=Rt[j][:, eb * 512:(eb + 1) * 512], t=tmp:
                             e.tensor_add(o, o, t), EV)
                b.wait("gpsimd", EV)
                for j in range(4):
                    b.dma("gpsimd", SW, R_hbm[t0 + j * 128:t0 + (j + 1) * 128, :], Rt[j])
                b.wait("gpsimd", SW)
                b.barrier(PH)

        if "s6" in phases:
            s6_prep()
            emit_s6()

        def emit_s7():
            for dk in range(8):
                b.dma("gpsimd", SW, AW(dk * 3072, 3072), s7_qkv[dk * 128:(dk + 1) * 128, :])
                b.dma("gpsimd", SW, AW(24576 + dk * 256, 256), s7_rq[dk * 128:(dk + 1) * 128, :])
                b.dma("gpsimd", SW, AW(26624 + dk * 512, 512), s7_rkv[dk * 128:(dk + 1) * 128, :])
            for ck in range(2):
                b.dma("gpsimd", SW, AW(30720 + ck * 1024, 1024), s7_rout[ck * 128:(ck + 1) * 128, :])
            for dk in range(8):
                b.dma("gpsimd", SW, A16(16384 + dk * 1024, 1024), s7_out[dk * 128:(dk + 1) * 128, :])
            b.dma("gpsimd", SW, rowsb[0:1, 0:1024], s7_vb_row)
            b.dma("gpsimd", SW, rowsb[0:1, 1024:2048], s7_routb_row)
            b.dma("gpsimd", SW, rowsb[0:1, 2048:3072], s7_outb_row)
            b.dma("sync", LD, cols[:, 72:80], s7_qb.rearrange("c p one -> p (c one)"),
                  allow_slow_non_contiguous=True)
            b.dma("sync", LD, cols[:, 80:88], s7_kb.rearrange("c p one -> p (c one)"),
                  allow_slow_non_contiguous=True)
            b.dma("sync", LD, cols[:, 88:90], s7_rqb.rearrange("c p one -> p (c one)"),
                  allow_slow_non_contiguous=True)
            b.dma("sync", LD, cols[:, 90:94], s7_rkvb.rearrange("c p one -> p (c one)"),
                  allow_slow_non_contiguous=True)
            b.dma("sync", LD, A32(2816, 64), lat_iota)
            b.wait("gpsimd", SW)
            b.wait("vector", LD)
            b.op("vector", lambda e, o=cols[:, 72:80]: e.tensor_scalar(
                o, o, 0.125, None, ALU.mult))
            b.op("vector", lambda e, o=cols[:, 88:90]: e.tensor_scalar(
                o, o, 0.125, None, ALU.mult))
            b.op("vector", lambda e: e.drain())
            b.op("vector", lambda e: e.memset(a16[:, 4096:12288], 0.0))
            b.op("vector", lambda e: e.drain())
            b.op("vector", lambda e: e.memset(a16[:, 14080:15248], 0.0))
            b.op("vector", lambda e: e.drain())
            b.op("vector", lambda e: e.engine_nop(), EV)
            b.wait("gpsimd", EV)
            b.dma("gpsimd", SW, A32(3904, 1024),
                  s7_glgr_row[0:1, 0:1024].squeeze(0).partition_broadcast(128))
            b.dma("gpsimd", SW, A32(4928, 1024),
                  s7_glgr_row[0:1, 1024:2048].squeeze(0).partition_broadcast(128))
            b.wait("gpsimd", SW)
            b.barrier(PH)
            glmat, grmat = A32(3904, 1024), A32(4928, 1024)
            NC7 = S // 256
            for ch in range(NC7):
                t0 = ch * 256
                par, prv = ch % 2, 1 - ch % 2
                kFo = [4096, 6144]
                vTo = [8192, 10240]
                Rt = [A32(j * 1024, 1024) for j in range(2)]
                for j in range(2):
                    b.dma("sync", LD, Rt[j], R_hbm[t0 + j * 128:t0 + (j + 1) * 128, :])
                if ch <= 1:
                    for j in range(2):
                        b.dma("sync", LD, A32(2048 + j * 384, 384),
                              loc_tab[min(ch, 1)][j * 128:(j + 1) * 128, :])
                b.wait("vector", LD)
                b.wait("scalar", LD)
                for j in range(2):
                    xn1 = A32(2880, 1024)
                    emit_ln(Rt[j], xn1, 1024)
                    st16 = A16(12288, 1024)
                    b.wait("vector", EA)
                    b.wait("vector", TP)
                    b.op("vector", lambda e, o=st16, s=xn1: e.tensor_copy(o, s), EV)
                    b.wait("scalar", EV)
                    b.dma_tr("scalar", TP,
                             A16(0, 2048).rearrange("p (c n) -> p c n", c=8)[:, :, j * 128:(j + 1) * 128],
                             st16.rearrange("p (c n) -> p c n", c=8))
                b.wait("tensor", TP)
                b.wait("vector", TP)
                pooled32 = A32(7504, 16)
                for dk in range(8):
                    for j in range(2):
                        b.op("vector", lambda e, o=pooled32[:, dk * 2 + j:dk * 2 + j + 1],
                             s=a16[:, dk * 256 + j * 128:dk * 256 + (j + 1) * 128]:
                             e.tensor_reduce(o, s, AX, ALU.add))
                b.op("vector", lambda e: e.drain())
                b.op("vector", lambda e, o=pooled32: e.tensor_scalar(
                    o, o, 2.0 ** -7, None, ALU.mult))
                b.op("vector", lambda e: e.drain())
                p16 = A16(15232, 16)
                b.op("vector", lambda e, o=p16, s=pooled32: e.tensor_copy(o, s), EV)
                b.wait("tensor", EV)
                for oc in range(4):
                    bank = psB[:, 0:2]
                    if oc > 0:
                        b.wait("tensor", EA)
                    for dk in range(8):
                        b.op("tensor", lambda e, o=bank,
                             l=AW(26624 + dk * 512 + oc * 128, 128),
                             r=p16[:, dk * 2:dk * 2 + 2], s=(dk == 0), p=(dk == 7):
                             e.matmul(o, l, r, start=s, stop=p), MM if dk == 7 else None)
                    b.wait("scalar", MM)
                    if oc < 2:
                        dst = a16[:, 14080 + oc * 64 + 2 * ch:14080 + oc * 64 + 2 * ch + 2]
                    else:
                        dst = a16[:, 14208 + (oc - 2) * 128 + 2 * ch:14208 + (oc - 2) * 128 + 2 * ch + 2]
                    b.op("scalar", lambda e, o=dst, i=bank, bi=cols[:, 90 + oc:91 + oc]:
                         e.activation(out=o, in_=i, func=AF.Identity, bias=bi), EA)
                def projF7(wbase, ocbase, noc, dsto, biascol, scale=None):
                    for oc in range(noc):
                        bank = psB[:, (oc % 4) * 512:(oc % 4) * 512 + 256]
                        b.wait("tensor", EA)
                        for dk in range(8):
                            b.op("tensor", lambda e, o=bank,
                                 l=AW(wbase + dk * (3072 if wbase == 0 else 256) + (ocbase + oc) * 128, 128),
                                 r=a16[:, dk * 256:(dk + 1) * 256],
                                 s=(dk == 0), p=(dk == 7):
                                 e.matmul(o, l, r, start=s, stop=p), MM if dk == 7 else None)
                        b.wait("scalar", MM)
                        if scale is None:
                            b.op("scalar", lambda e, o=a16[:, dsto + oc * 256:dsto + (oc + 1) * 256],
                                 i=bank, bi=cols[:, biascol + oc:biascol + oc + 1]:
                                 e.activation(out=o, in_=i, func=AF.Identity, bias=bi), EA)
                        else:
                            b.op("scalar", lambda e, o=a16[:, dsto + oc * 256:dsto + (oc + 1) * 256],
                                 i=bank, bi=cols[:, biascol + oc:biascol + oc + 1], sc=scale:
                                 e.activation(out=o, in_=i, func=AF.Identity, bias=bi, scale=sc), EA)
                projF7(0, 0, 8, 2048, 72, 0.125)
                projF7(0, 8, 8, kFo[par], 80)
                projF7(24576, 0, 2, 14720, 88, 0.125)
                for j in range(2):
                    for vh in range(2):
                        bank = psS[:, (j * 2 + vh) * 512:(j * 2 + vh) * 512 + 512]
                        b.wait("tensor", EV)
                        for dk in range(8):
                            b.op("tensor", lambda e, o=bank,
                                 l=a16[:, dk * 256 + j * 128:dk * 256 + (j + 1) * 128],
                                 r=AW(dk * 3072 + 2048 + vh * 512, 512),
                                 s=(dk == 0): e.matmul(o, l, r, start=s, stop=False))
                        b.op("tensor", lambda e, o=bank,
                             r=rowsb[0:1, vh * 512:(vh + 1) * 512]:
                             e.matmul(o, onesb[0:1, :], r, start=False, stop=True), MM)
                        b.wait("vector", MM)
                        b.op("vector", lambda e, o=a16[:, vTo[par] + j * 1024 + vh * 512:
                                                       vTo[par] + j * 1024 + (vh + 1) * 512],
                             i=bank: e.tensor_copy(o, i), EV)
                b.wait("scalar", EA)
                b.wait("scalar", EV)
                for ck in range(2):
                    b.dma_tr("scalar", TP, A16(14464 + ck * 128, 128),
                             A16(14208 + ck * 128, 128))
                b.wait("tensor", TP)
                b.wait("vector", TP)
                b.barrier(PH)
                for j in range(2):
                    cid = 2 * ch + j
                    locbank = psS[:, (2 * j) * 512:(2 * j) * 512 + 1024]
                    mm_sc, mm_pv, ev_stt, tp_h = {}, {}, {}, {}
                    P7off = [13696, 15248]
                    PToff = [13312, 15632]

                    def s7_scores(h):
                        p = h % 2
                        hp = 64 * (h % 2)
                        scb = psB[:, 1024 + p * 512:1024 + p * 512 + 384]
                        if h >= 2:
                            b.wait("tensor", EV, ev_stt[h - 2])
                        srcs = [(kFo[prv] + (h // 2) * 256 + 128, prv),
                                (kFo[par] + (h // 2) * 256, par),
                                (kFo[par] + (h // 2) * 256 + 128, par)]
                        for kt in range(3):
                            b.op("tensor", lambda e, o=scb[:, kt * 128:(kt + 1) * 128],
                                 l=a16[hp:hp + 64, 2048 + (h // 2) * 256 + j * 128:
                                       2048 + (h // 2) * 256 + (j + 1) * 128],
                                 r=a16[hp:hp + 64, srcs[kt][0]:srcs[kt][0] + 128]:
                                 e.matmul(o, l, r, start=True, stop=True),
                                 MM if kt == 2 else None)
                        mm_sc[h] = b.cnt[MM.name]
                        s32 = A32(7056 + p * 384, 384)
                        nm = cols[:, 104:105]
                        rs = cols[:, 105:106]
                        P7 = A16(P7off[p], 384)
                        b.wait("vector", MM, mm_sc[h])
                        b.op("vector", lambda e, o=s32, i=scb, t=A32(2048 + j * 384, 384):
                             e.tensor_add(o, i, t), EV)
                        ev_stt[h] = b.cnt[EV.name]
                        b.op("vector", lambda e: e.drain())
                        b.op("vector", lambda e, o=nm, s=s32: e.tensor_reduce(
                            o, s, AX, ALU.max, negate=True), EV)
                        b.wait("scalar", EV)
                        if h >= 2:
                            b.wait("scalar", TPa if p == 0 else TPb, tp_h[h - 2])
                        b.op("scalar", lambda e, o=P7, s=s32, m=nm: e.activation(
                            out=o, in_=s, func=AF.Exp, bias=m), EA)
                        b.wait("vector", EA)
                        b.op("vector", lambda e, o=rs, s=P7: e.tensor_reduce(o, s, AX, ALU.add))
                        b.op("vector", lambda e: e.drain())
                        b.op("vector", lambda e, o=rs: e.reciprocal(o, o))
                        b.op("vector", lambda e: e.drain())
                        if h >= 2:
                            b.wait("vector", TPa if p == 0 else TPb, tp_h[h - 2])
                        b.op("vector", lambda e, o=P7, r=rs: e.tensor_scalar(
                            o, o, r, None, ALU.mult), EV)
                        if ch == 0 and j == 0:
                            b.op("vector", lambda e: e.drain())
                            b.op("vector", lambda e, o=P7[0:1, :]: e.tensor_scalar(
                                o, o, 0.0, None, ALU.mult), EV)
                        b.wait("scalar", EV)
                        if h >= 2:
                            b.wait("scalar", MM, mm_pv[h - 2])
                            b.wait("scalar", TPa if p == 0 else TPb, tp_h[h - 2])
                        b.dma_tr("scalar", TPa if p == 0 else TPb,
                                 A16(PToff[p], 384).rearrange("p (c n) -> p c n", c=3),
                                 P7.rearrange("p (c n) -> p c n", c=3))
                        tp_h[h] = b.cnt[(TPa if p == 0 else TPb).name]

                    def s7_pv(h):
                        p = h % 2
                        b.wait("tensor", TPa if p == 0 else TPb, tp_h[h])
                        vsrcs = [vTo[prv] + 1024, vTo[par], vTo[par] + 1024]
                        for kt in range(3):
                            b.op("tensor", lambda e,
                                 o=locbank[:, h * 64:(h + 1) * 64],
                                 l=A16(PToff[p] + kt * 128, 128),
                                 r=a16[:, vsrcs[kt] + h * 64:vsrcs[kt] + (h + 1) * 64],
                                 s=(kt == 0), p2=(kt == 2):
                                 e.matmul(o, l, r, start=s, stop=p2), MM if kt == 2 else None)
                        mm_pv[h] = b.cnt[MM.name]

                    s7_scores(0)
                    for h in range(1, 16):
                        s7_scores(h)
                        s7_pv(h - 1)
                    s7_pv(15)
                    latbank = psS[:, (2 * (1 - j)) * 512:(2 * (1 - j)) * 512 + 256]
                    lsb = psB[:, 1024:1088]
                    b.wait("tensor", EV)
                    b.wait("tensor", TPa)
                    b.wait("tensor", TPb)
                    if cid > 0:
                        cmpm = A32(6992, 64)
                        b.wait("vector", MM)
                        b.op("vector", lambda e, o=cmpm, s=A32(2816, 64), c=float(cid):
                             e.tensor_scalar(o, s, c, -1e30, ALU.is_ge, ALU.mult), EV)
                        b.op("vector", lambda e: e.drain())
                        b.op("vector", lambda e, o=A16(13696 + 64, 64): e.memset(o, 0.0), EV)
                        for lh in range(4):
                            lhp = 64 * (lh % 2)
                            b.wait("tensor", EV)
                            b.op("tensor", lambda e, o=lsb,
                                 l=a16[lhp:lhp + 64, 14720 + (lh // 2) * 256 + j * 128:
                                       14720 + (lh // 2) * 256 + (j + 1) * 128],
                                 r=a16[lhp:lhp + 64, 14080 + (lh // 2) * 64:
                                       14080 + (lh // 2) * 64 + 64]:
                                 e.matmul(o, l, r, start=True, stop=True), MM)
                            b.wait("vector", MM)
                            s32l = A32(7440, 64)
                            b.op("vector", lambda e, o=s32l, i=lsb, t=cmpm:
                                 e.tensor_add(o, i, t))
                            b.op("vector", lambda e: e.drain())
                            nm = cols[:, 104:105]
                            rs = cols[:, 105:106]
                            b.op("vector", lambda e, o=nm, s=s32l: e.tensor_reduce(
                                o, s, AX, ALU.max, negate=True), EV)
                            b.wait("scalar", EV)
                            b.wait("scalar", TP)
                            Plat = A16(13696, 64)
                            b.op("scalar", lambda e, o=Plat, s=s32l, m=nm: e.activation(
                                out=o, in_=s, func=AF.Exp, bias=m), EA)
                            b.wait("vector", EA)
                            b.op("vector", lambda e, o=rs, s=Plat: e.tensor_reduce(
                                o, s, AX, ALU.add))
                            b.op("vector", lambda e: e.drain())
                            b.op("vector", lambda e, o=rs: e.reciprocal(o, o))
                            b.op("vector", lambda e: e.drain())
                            b.wait("vector", TP)
                            b.op("vector", lambda e, o=Plat, r=rs: e.tensor_scalar(
                                o, o, r, None, ALU.mult), EV)
                            b.wait("scalar", EV)
                            b.wait("scalar", MM)
                            b.dma_tr("scalar", TP, A16(13312, 128), A16(13696, 128))
                            b.wait("tensor", TP)
                            b.op("tensor", lambda e, o=latbank[:, lh * 64:(lh + 1) * 64],
                                 l=A16(13312, 128),
                                 r=A16(14464 + (lh // 2) * 128 + (lh % 2) * 64, 64):
                                 e.matmul(o, l, r, start=True, stop=True), MM)
                    if cid > 0:
                        lat16 = A16(13312, 256)
                        b.wait("vector", MM)
                        b.wait("vector", TP)
                        b.op("vector", lambda e, o=lat16, i=latbank: e.tensor_copy(o, i), EV)
                        b.wait("scalar", EV)
                        b.wait("scalar", MM)
                        b.dma_tr("scalar", TP,
                                 A16(13696, 256).rearrange("p (c n) -> p c n", c=2),
                                 lat16.rearrange("p (c n) -> p c n", c=2))
                        b.wait("tensor", TP)
                    for eb in range(2):
                        rbank = psB[:, eb * 512:eb * 512 + 512]
                        b.wait("tensor", EV)
                        b.wait("tensor", EA)
                        if cid > 0:
                            for ck in range(2):
                                b.op("tensor", lambda e, o=rbank, l=A16(13696 + ck * 128, 128),
                                     r=AW(30720 + ck * 1024 + eb * 512, 512),
                                     s=(ck == 0): e.matmul(o, l, r, start=s, stop=False))
                        b.op("tensor", lambda e, o=rbank,
                             r=rowsb[0:1, 1024 + eb * 512:1024 + (eb + 1) * 512],
                             s=(cid == 0): e.matmul(o, onesb[0:1, :], r, start=s, stop=True),
                             MM)
                    b.wait("vector", MM)
                    lmul = A32(5952, 1024)
                    rmt32 = A32(7552, 1024)
                    gate16 = A32(6976, 16)
                    b.op("vector", lambda e, o=rmt32, i=psB[:, 0:1024]: e.tensor_copy(o, i))
                    b.op("vector", lambda e: e.drain())
                    b.op("vector", lambda e, o=lmul, i=locbank, g2=glmat:
                         e.tensor_mul(o, i, g2))
                    b.op("vector", lambda e: e.drain())
                    b.op("vector", lambda e, o=gate16, s=lmul.rearrange("p (h d) -> p h d", h=16):
                         e.tensor_reduce(o, s, AX, ALU.add))
                    b.op("vector", lambda e: e.drain())
                    b.op("vector", lambda e, o=lmul, i=rmt32, g2=grmat:
                         e.tensor_mul(o, i, g2))
                    b.op("vector", lambda e: e.drain())
                    b.op("vector", lambda e, o=A32(7056, 16),
                         s=lmul.rearrange("p (h d) -> p h d", h=16):
                         e.tensor_reduce(o, s, AX, ALU.add))
                    b.op("vector", lambda e: e.drain())
                    b.op("vector", lambda e, o=gate16, t=A32(7056, 16):
                         e.tensor_add(o, o, t))
                    b.op("vector", lambda e: e.drain())
                    b.op("vector", lambda e: e.engine_nop(), EV)
                    b.wait("scalar", EV)
                    b.op("scalar", lambda e, o=gate16: e.activation(
                        out=o, in_=o, func=AF.Sigmoid), EA)
                    b.wait("vector", EA)
                    b.wait("vector", TP)
                    mx16 = A16(12288, 1024)
                    t1 = A32(6992, 64)
                    for h in range(16):
                        b.op("vector", lambda e, o=t1, i=locbank[:, h * 64:(h + 1) * 64],
                             r=rmt32[:, h * 64:(h + 1) * 64]: e.tensor_sub(o, i, r))
                        b.op("vector", lambda e: e.drain())
                        b.op("vector", lambda e, o=mx16[:, h * 64:(h + 1) * 64], i=t1,
                             g2=gate16[:, h:h + 1], r=rmt32[:, h * 64:(h + 1) * 64]:
                             e.scalar_tensor_tensor(o, i, g2, r, ALU.mult, ALU.add))
                        b.op("vector", lambda e: e.drain())
                    b.op("vector", lambda e: e.engine_nop(), EV)
                    b.wait("scalar", EV)
                    b.wait("scalar", MM)
                    b.dma_tr("scalar", TP,
                             A16(15248, 1024).rearrange("p (c n) -> p c n", c=8),
                             mx16.rearrange("p (c n) -> p c n", c=8))
                    b.wait("tensor", TP)
                    for eb in range(2):
                        bank = psB[:, eb * 512:(eb + 1) * 512]
                        b.wait("tensor", EV)
                        for dk in range(8):
                            b.op("tensor", lambda e, o=bank, l=A16(15248 + dk * 128, 128),
                                 r=A16(16384 + dk * 1024 + eb * 512, 512),
                                 s=(dk == 0): e.matmul(o, l, r, start=s, stop=False))
                        b.op("tensor", lambda e, o=bank,
                             r=rowsb[0:1, 2048 + eb * 512:2048 + (eb + 1) * 512]:
                             e.matmul(o, onesb[0:1, :], r, start=False, stop=True), MM)
                        b.wait("vector", MM)
                        b.op("vector", lambda e, o=Rt[j][:, eb * 512:(eb + 1) * 512],
                             i=bank: e.tensor_add(o, o, i), EV)
                b.wait("gpsimd", EV)
                for j in range(2):
                    b.dma("gpsimd", SW, R_hbm[t0 + j * 128:t0 + (j + 1) * 128, :], Rt[j])
                b.wait("gpsimd", SW)
                b.barrier(PH)

        if "s7" in phases:
            emit_s7()

        if "s8" in phases:
            if "s6" not in phases:
                s6_prep()
            emit_s6()

        b.dma("gpsimd", ST, y, R_hbm)
        b.wait("gpsimd", ST)

        b.replay(block)
    return nc


def kernel(**inputs):
    import os
    S = int(np.asarray(inputs["L0_attn_a_x"]).shape[1])
    ph = tuple(p for p in os.environ.get("BASS_PHASES", ",".join(ALL_PHASES)).split(",") if p)
    nc = build_nc(S=S, phases=ph)
    consts = host_consts(S)
    in_maps = []
    for c in range(8):
        m = host_layout(inputs, lane=c % 4)
        m.update(consts)
        in_maps.append(m)
    res = run_bass_kernel_spmd(nc, in_maps, core_ids=list(range(8)))
    out = np.stack([res.results[c]["y"] for c in range(4)], axis=0)
    return out.astype(np.float32)
